# version 27
# baseline (speedup 1.0000x reference)
import numpy as np
from contextlib import ExitStack

import concourse.bass as bass
import concourse.mybir as mybir
from concourse.alu_op_type import AluOpType as A
from concourse.bass_utils import run_bass_kernel_spmd

F32 = mybir.dt.float32
BF16 = mybir.dt.bfloat16
F16 = mybir.dt.float16
FP8 = mybir.dt.float8e4
AF = mybir.ActivationFunctionType
AX = mybir.AxisListType.X

P = 128
D = 1024
DC = D // P
SEQ = 2048
NT = SEQ // P
NCH = 4
TPC = 4
NSEQ = 4
INNER = 4096
IC = INNER // P
ALPHA = 0.99

C_EPS, C_1536, C_QSC = 0, 1, 2


def build_nc():
    nc = bass.Bass("TRN2", target_bir_lowering=False)

    x_in = nc.dram_tensor("x", [NSEQ, SEQ, D], F32, kind="ExternalInput").ap()
    wqkvT = nc.dram_tensor("wqkvT", [D, 3 * D], F32, kind="ExternalInput").ap()
    bq_in = nc.dram_tensor("bq", [3 * D], F32, kind="ExternalInput").ap()
    wgT = nc.dram_tensor("wgT", [D, INNER], F32, kind="ExternalInput").ap()
    wuT = nc.dram_tensor("wuT", [D, INNER], F32, kind="ExternalInput").ap()
    wdT = nc.dram_tensor("wdT", [INNER, D], F32, kind="ExternalInput").ap()
    maskT_in = nc.dram_tensor("maskT", [P, NT * P], F32, kind="ExternalInput").ap()
    ident_in = nc.dram_tensor("ident", [P, P], F32, kind="ExternalInput").ap()
    consts_in = nc.dram_tensor("consts", [P, 8], F32, kind="ExternalInput").ap()

    out = nc.dram_tensor("out", [NSEQ, SEQ, D], F32, kind="ExternalOutput").ap()
    x2s = nc.dram_tensor("x2s", [NSEQ, SEQ, D], F32, kind="ExternalOutput").ap()
    wg8h = nc.dram_tensor("wg8h", [P, DC * INNER], FP8, kind="ExternalOutput").ap()
    wu8h = nc.dram_tensor("wu8h", [P, DC * INNER], FP8, kind="ExternalOutput").ap()
    wd8h = nc.dram_tensor("wd8h", [P, IC * D], FP8, kind="ExternalOutput").ap()

    xv = x_in.rearrange("s (t p) d -> s p t d", p=P)
    outv = out.rearrange("s (t p) d -> s p t d", p=P)
    x2v = x2s.rearrange("s (t p) d -> s p t d", p=P)
    wqkvTv = wqkvT.rearrange("(c p) o -> c p o", p=P)
    wgTv = wgT.rearrange("(c p) o -> c p o", p=P)
    wuTv = wuT.rearrange("(c p) o -> c p o", p=P)
    wdTv = wdT.rearrange("(c p) o -> c p o", p=P)
    bqv = bq_in.rearrange("(c p) -> p c", p=P)
    bv_v = bq_in.rearrange("(a o) -> a o", a=3)[2:3, :]

    sems = {}
    ctrs = {}

    def INC(inst, name, k=1):
        inst.then_inc(sems[name], k)
        ctrs[name] += k
        return ctrs[name]

    def W(eng, name, v):
        if v > 0:
            eng.wait_ge(sems[name], v)

    LD_LANES = ["ld0", "ld1", "ld2", "ld3"]
    GP_LANE = "gld"
    ST_LANES = ["st0", "st1", "st2"]
    lane_rr = {"ld": 0, "st": 0}

    with ExitStack() as ctx:
        for s_ in ("pe", "dve", "act", GP_LANE, *LD_LANES, *ST_LANES):
            sems[s_] = ctx.enter_context(nc.semaphore("sem_" + s_))
            ctrs[s_] = 0

        def DMA(eng, kind, out_, in_, **kw):
            lanes = LD_LANES if kind == "ld" else ST_LANES
            i = lane_rr[kind]
            lane_rr[kind] = (i + 1) % len(lanes)
            ln_ = lanes[i]
            W(eng, ln_, ctrs[ln_])
            eng.dma_start(out=out_, in_=in_, **kw).then_inc(sems[ln_], 16)
            ctrs[ln_] += 16
            return (ln_, ctrs[ln_])

        def WE(eng, ev):
            if ev:
                W(eng, ev[0], ev[1])

        def W2(eng, ev):
            if ev:
                if isinstance(ev, list):
                    for e in ev:
                        W(eng, e[0], e[1])
                else:
                    W(eng, ev[0], ev[1])

        def SNAP(kind):
            lanes = LD_LANES if kind == "ld" else ST_LANES
            return [(ln_, ctrs[ln_]) for ln_ in lanes if ctrs[ln_] > 0]

        blk = ctx.enter_context(nc.Block())

        consts = ctx.enter_context(nc.sbuf_tensor("consts_sb", [P, 8], F32))
        ident_b = ctx.enter_context(nc.sbuf_tensor("ident_b", [P, P], BF16))
        bq_pc = ctx.enter_context(nc.sbuf_tensor("bq_pc", [P, 24], F32))
        bq32 = ctx.enter_context(nc.sbuf_tensor("bq32", [P, 8], F32))
        bv_b = ctx.enter_context(nc.sbuf_tensor("bv_b", [1, D], BF16))
        ones_col = ctx.enter_context(nc.sbuf_tensor("ones_col", [P, 1], F32))
        ones_rowf = ctx.enter_context(nc.sbuf_tensor("ones_rowf", [1, P], F32))
        ones_rowb = ctx.enter_context(nc.sbuf_tensor("ones_rowb", [1, P], BF16))
        swb = ctx.enter_context(nc.sbuf_tensor("swb", [P, 8], F32))
        sw_pack = ctx.enter_context(nc.sbuf_tensor("sw_pack", [1, 8], F32))
        stt_ = ctx.enter_context(nc.sbuf_tensor("stats", [P, 224], F32))
        sqs = ctx.enter_context(nc.sbuf_tensor("sqs", [P, D], F32))

        NSTAT = 24

        def SC(par, name, t=None, n=4):
            base = 32 + par * NSTAT * 4 + STAT_OFF[name] * 4
            if t is None:
                return stt_[:, base:base + n]
            return stt_[:, base + t:base + t + 1]
        STAT_OFF = {k: i for i, k in enumerate(
            ["sum", "sumsq", "m", "m2", "var", "sd", "r", "mb", "mx", "mn",
             "a1", "a2", "Mx", "s1", "is1", "rs1", "b2", "c1", "c2",
             "hmx", "s2", "is2", "c3"])}

        pT0 = ctx.enter_context(nc.psum_tensor("pT0", [P, 1024], BF16))
        pT1 = ctx.enter_context(nc.psum_tensor("pT1", [P, 1024], BF16))
        pS0 = ctx.enter_context(nc.psum_tensor("pS0", [P, 512], F32))
        pS1 = ctx.enter_context(nc.psum_tensor("pS1", [P, 512], F32))
        pA = ctx.enter_context(nc.psum_tensor("pA", [P, 512], F32))
        pB = ctx.enter_context(nc.psum_tensor("pB", [P, 512], F32))
        pC = ctx.enter_context(nc.psum_tensor("pC", [P, 512], F32))
        pD = ctx.enter_context(nc.psum_tensor("pD", [P, 512], F32))

        ev_init = []

        def init_loads(eng):
            ev_init.append(DMA(eng, "ld", consts[:], consts_in))
            ev_init.append(DMA(eng, "ld", bq_pc[:], bqv, allow_slow_non_contiguous=True))
            ev_init.append(DMA(eng, "ld", sqs[0:1, 0:D], bv_v))
        blk.sync(init_loads)

        def init_gp(eng):
            eng.dma_start(out=ident_b[:], in_=ident_in).then_inc(sems[GP_LANE], 16)
            ctrs[GP_LANE] += 16
            ev_init.append((GP_LANE, ctrs[GP_LANE]))
        blk.gpsimd(init_gp)

        def init_v(eng):
            eng.memset(ones_col[:], 1.0)
            eng.memset(ones_rowf[:], 1.0)
            eng.memset(ones_rowb[:], 1.0)
            W2(eng, ev_init)
            eng.tensor_copy(bv_b[:], sqs[0:1, 0:D])
            INC(eng.tensor_scalar(out=bq32[:], in0=bq_pc[:, 0:8], scalar1=0.03125,
                                  scalar2=None, op0=A.mult), "dve", 1)
        blk.vector(init_v)
        ev_init_v = ctrs["dve"]

        with ExitStack() as pa:
            maskT = pa.enter_context(nc.sbuf_tensor("maskT_sb", [P, NT * P], F32))
            wbuf = [pa.enter_context(nc.sbuf_tensor(f"wbuf{i}", [P, D], F32)) for i in range(2)]
            y16p = pa.enter_context(nc.sbuf_tensor("y16p", [P, D], F16))
            y16q = pa.enter_context(nc.sbuf_tensor("y16q", [P, D], F16))
            q8b = [pa.enter_context(nc.sbuf_tensor(f"q8_{i}", [P, D], FP8)) for i in range(2)]
            absum = pa.enter_context(nc.sbuf_tensor("absum", [P, 96], F32))
            wq_sb = pa.enter_context(nc.sbuf_tensor("wq_sb", [P, DC * 3 * D], BF16))
            kT = pa.enter_context(nc.sbuf_tensor("kT", [P, DC * SEQ], BF16))
            v_s = pa.enter_context(nc.sbuf_tensor("v_s", [P, NT * D], BF16))
            qT = pa.enter_context(nc.sbuf_tensor("qT", [P, DC * 512], BF16))
            xlnT = pa.enter_context(nc.sbuf_tensor("xlnT", [P, DC * 512], BF16))
            xln = [pa.enter_context(nc.sbuf_tensor(f"xln{i}", [P, D], BF16)) for i in range(4)]
            xc = [pa.enter_context(nc.sbuf_tensor(f"xc{i}", [P, TPC * D], F32)) for i in range(2)]
            x2c = pa.enter_context(nc.sbuf_tensor("x2c", [P, 2 * D], F32))
            scb = [pa.enter_context(nc.sbuf_tensor(f"scb{i}", [P, 256], BF16)) for i in range(2)]

            ev_wq = {}
            for c in range(DC):
                lev = []

                def ldw(eng, c=c, lev=lev):
                    if c >= 2:
                        W(eng, "dve", ev_wq.get(c - 2, 0))
                    lev.append(DMA(eng, "ld", xc[c % 2][:, 0:3 * D], wqkvTv[c]))
                blk.sync(ldw)

                def cvw(eng, c=c, lev=lev):
                    W2(eng, lev)
                    INC(eng.tensor_copy(wq_sb[:, c * 3 * D:(c + 1) * 3 * D], xc[c % 2][:, 0:3 * D]), "dve", 1)
                blk.vector(cvw)
                ev_wq[c] = ctrs["dve"]
            ev_wq_done = ctrs["dve"]
            ev_mask = []
            blk.sync(lambda e: ev_mask.append(DMA(e, "ld", maskT[:], maskT_in)))

            ev_xc = {}
            ev_x2st = {}
            ev_pe = {}
            ev_ev = {}
            NQ = NSEQ * NCH

            NSUB = 96

            def p0_view(k):
                if k < 32:
                    wi, dc, qt = 0, k // 4, k % 4
                    return wgTv[dc][:, qt * D:(qt + 1) * D], wi, wg8h, dc * INNER + qt * D
                if k < 64:
                    k -= 32
                    wi, dc, qt = 1, k // 4, k % 4
                    return wuTv[dc][:, qt * D:(qt + 1) * D], wi, wu8h, dc * INNER + qt * D
                ic = k - 64
                return wdTv[ic], 2, wd8h, ic * D

            p0 = {"k1": 0, "k2": 0, "sw": False, "ld1": {}, "ld2": {}, "red": {},
                  "q8": {}, "st": {}, "swv": 0, "swb": 0}

            def p0_load(pass_, k):
                if k >= NSUB:
                    return
                tv = p0_view(k)[0]
                lds = p0["ld1"] if pass_ == 1 else p0["ld2"]

                def ld(eng, k=k, tv=tv, pass_=pass_):
                    if pass_ == 1:
                        W(eng, "dve", p0["red"].get(k - 2, 0))
                    else:
                        W(eng, "dve", p0["q8"].get(k - 2, p0["red"].get(NSUB - 2 + (k % 2), 0)))
                    lds[k] = DMA(eng, "ld", wbuf[k % 2][:], tv)
                blk.sync(ld)

            def p0_unit():
                if p0["k1"] < NSUB:
                    k = p0["k1"]
                    p0["k1"] += 1

                    def f(k=k):
                        def red(eng, k=k):
                            W2(eng, p0["ld1"][k])
                            INC(eng.tensor_reduce(out=absum[:, k:k + 1], in_=wbuf[k % 2][:],
                                                  axis=AX, op=A.add,
                                                  apply_absolute_value=True), "dve", 1)
                            p0["red"][k] = ctrs["dve"]
                        blk.vector(red)
                        p0_load(1, k + 2)
                    return ("emit", f)
                if not p0["sw"]:
                    return None
                if p0["k2"] < NSUB:
                    k = p0["k2"]
                    p0["k2"] += 1

                    def f(k=k):
                        tv, wi, dramout, off = p0_view(k)

                        def quant(eng, k=k, wi=wi):
                            W2(eng, p0["ld2"][k])
                            W(eng, "dve", p0["swb"])
                            W(eng, "dve", p0["q8"].get(k - 1, 0))
                            INC(eng.tensor_scalar(out=y16p[:], in0=wbuf[k % 2][:],
                                                  scalar1=swb[:, 2 * wi:2 * wi + 1], scalar2=1536.0,
                                                  op0=A.mult, op1=A.add), "dve", 1)
                            W(eng, "dve", ctrs["dve"])
                            INC(eng.tensor_scalar(out=y16q[:], in0=y16p[:], scalar1=-1536.0,
                                                  scalar2=-1.0, op0=A.add, op1=A.max), "dve", 1)
                            W2(eng, p0["st"].get(k - 2, None))
                            W(eng, "dve", ctrs["dve"])
                            INC(eng.tensor_scalar(out=q8b[k % 2][:], in0=y16q[:], scalar1=1.0,
                                                  scalar2=None, op0=A.min), "dve", 1)
                            p0["q8"][k] = ctrs["dve"]
                        blk.vector(quant)

                        def stq(eng, k=k, dramout=dramout, off=off):
                            W(eng, "dve", p0["q8"][k])
                            p0["st"][k] = DMA(eng, "st", dramout[:, off:off + D], q8b[k % 2][:])
                        blk.scalar(stq)
                        p0_load(2, k + 2)
                    return ("emit", f)
                return None

            def p0_sw_chain(q):
                def sw_pe(eng, q=q):
                    W(eng, "dve", p0["red"][NSUB - 1])
                    W(eng, "dve", ev_ev.get(("sc_last",) + sc_of(q), 0))
                    INC(eng.matmul(pS0[0:1, 0:96], ones_col[:], absum[:], start=True, stop=True), "pe", 1)
                blk.tensor(sw_pe)
                ev_red = ctrs["pe"]

                def sw_v(eng):
                    W(eng, "pe", ev_red)
                    for wi in range(3):
                        eng.tensor_reduce(out=sw_pack[0:1, 2 * wi + 1:2 * wi + 2],
                                          in_=pS0[0:1, 32 * wi:32 * wi + 32], axis=AX, op=A.add)
                    eng.drain()
                    for wi in range(3):
                        eng.tensor_scalar(out=sw_pack[0:1, 2 * wi + 1:2 * wi + 2],
                                          in0=sw_pack[0:1, 2 * wi + 1:2 * wi + 2],
                                          scalar1=float(2.0 ** -22), scalar2=1e-5,
                                          op0=A.mult, op1=A.max)
                    eng.drain()
                    for wi in range(3):
                        eng.reciprocal(out=sw_pack[0:1, 2 * wi:2 * wi + 1],
                                       in_=sw_pack[0:1, 2 * wi + 1:2 * wi + 2])
                    eng.memset(sw_pack[0:1, 7:8], 0.0)
                    eng.tensor_scalar(out=sw_pack[0:1, 6:7], in0=sw_pack[0:1, 5:6],
                                      scalar1=float(1.0 / 127.0), scalar2=None, op0=A.mult)
                    INC(eng.drain(), "dve", 1)
                blk.vector(sw_v)
                p0["swv"] = ctrs["dve"]

                def sw_bc(eng):
                    W(eng, "dve", p0["swv"])
                    INC(eng.matmul(pS1[:, 0:8], ones_rowf[:], sw_pack[:], start=True, stop=True), "pe", 1)
                blk.tensor(sw_bc)
                ev_swb_pe = ctrs["pe"]

                def sw_cp(eng):
                    W(eng, "pe", ev_swb_pe)
                    INC(eng.tensor_copy(swb[:], pS1[:, 0:8]), "dve", 1)
                blk.vector(sw_cp)
                p0["swb"] = ctrs["dve"]
                p0["sw"] = True

            def sc_of(q):
                return (q // NCH, q % NCH)

            def a_ldx(q):
                s, c = sc_of(q)
                par = c % 2

                def ldx(eng, s=s, c=c, par=par, q=q):
                    if q >= 2:
                        W(eng, "dve", ev_xc.get(("free",) + sc_of(q - 2), 0))
                    elif s == 0 and c < 2:
                        W(eng, "dve", ev_wq_done)
                    ev_xc[("ld", s, c)] = DMA(eng, "ld", xc[par][:].rearrange("p (t d) -> p t d", t=TPC),
                                              xv[s, :, 4 * c:4 * c + 4, :])
                blk.sync(ldx)

            def a_stats_units(q):
                s, c = sc_of(q)
                par = c % 2
                pk = sc_of(q - 2) if q >= 2 else None
                units = []

                def u_red(t, q=q, s=s, c=c, par=par, pk=pk):
                    def f(eng, t=t):
                        if t == 0:
                            W2(eng, ev_xc[("ld", s, c)])
                            if pk:
                                W(eng, "dve", ev_ev.get(("ln_done",) + pk, 0))
                                W(eng, "act", ev_ev.get(("sd",) + pk, 0))
                            prev1 = sc_of(q - 1) if q >= 1 else None
                            if prev1:
                                W(eng, "dve", ev_ev.get(("ln_done",) + prev1, 0))
                        xsl = xc[par][:, t * D:(t + 1) * D]
                        eng.tensor_reduce(out=SC(par, "sum", t), in_=xsl, axis=AX, op=A.add)
                        if t:
                            W(eng, "dve", ev_ev[("ur", s, c, t - 1)])
                        INC(eng.scalar_tensor_tensor(out=sqs[:], in0=xsl,
                                                     scalar=1.0, in1=xsl,
                                                     op0=A.mult, op1=A.mult,
                                                     accum_out=SC(par, "sumsq", t)), "dve", 1)
                        ev_ev[("ur", s, c, t)] = ctrs["dve"]
                    return f
                for t in range(TPC):
                    units.append(("dve", u_red(t)))

                def u_mv(eng, par=par, s=s, c=c):
                    W(eng, "dve", ev_ev[("ur", s, c, TPC - 1)])
                    INC(eng.tensor_scalar(out=SC(par, "m"), in0=SC(par, "sum"),
                                          scalar1=float(2.0 ** -10), scalar2=None, op0=A.mult), "dve", 1)
                    W(eng, "dve", ctrs["dve"])
                    INC(eng.tensor_tensor(out=SC(par, "m2"), in0=SC(par, "m"), in1=SC(par, "m"), op=A.mult), "dve", 1)
                    W(eng, "dve", ctrs["dve"])
                    INC(eng.scalar_tensor_tensor(out=SC(par, "var"), in0=SC(par, "sumsq"),
                                                 scalar=float(2.0 ** -10), in1=SC(par, "m2"),
                                                 op0=A.mult, op1=A.subtract), "dve", 1)
                units.append(("dve", u_mv))

                def u_sqrt_emit():
                    ev_var = ctrs["dve"]

                    def u_sqrt(eng, par=par, ev_var=ev_var):
                        W(eng, "dve", ev_var)
                        INC(eng.activation(SC(par, "sd"), SC(par, "var"), AF.Sqrt,
                                           bias=consts[:, C_EPS:C_EPS + 1], scale=1.0), "act", 1)
                    blk.scalar(u_sqrt)
                    ev_ev[("sd", s, c)] = ctrs["act"]
                units.append(("emit", u_sqrt_emit))

                def u_rmb(eng, par=par, s=s, c=c):
                    W(eng, "act", ev_ev[("sd", s, c)])
                    INC(eng.reciprocal(out=SC(par, "r"), in_=SC(par, "sd")), "dve", 1)
                    W(eng, "dve", ctrs["dve"])
                    INC(eng.scalar_tensor_tensor(out=SC(par, "mb"), in0=SC(par, "m"),
                                                 scalar=-1.0, in1=SC(par, "r"),
                                                 op0=A.mult, op1=A.mult), "dve", 1)
                    ev_ev[("rmb", s, c)] = ctrs["dve"]
                units.append(("dve", u_rmb))

                def u_xln(t, q=q, s=s, c=c, par=par):
                    def f(eng, t=t):
                        if t == 0:
                            W(eng, "dve", ev_ev[("rmb", s, c)])
                        if q >= 1:
                            W(eng, "pe", ev_pe.get(("xlnT_pe",) + sc_of(q - 1) + (t,), 0))
                        INC(eng.tensor_scalar(out=xln[t][:], in0=xc[par][:, t * D:(t + 1) * D],
                                              scalar1=SC(par, "r", t), scalar2=SC(par, "mb", t),
                                              op0=A.mult, op1=A.add), "dve", 1)
                        ev_ev[("xln", s, c, t)] = ctrs["dve"]
                        if t == TPC - 1:
                            ev_ev[("ln_done", s, c)] = ctrs["dve"]
                    return f
                for t in range(TPC):
                    units.append(("dve", u_xln(t)))
                return units

            def a_emit_units(units, n=99):
                k = 0
                while units and k < n:
                    kind, f = units.pop(0)
                    if kind == "emit":
                        f()
                    else:
                        blk.vector(f)
                    k += 1

            def a_transp(q, tiles=None):
                s, c = sc_of(q)
                for t in (tiles if tiles is not None else range(TPC)):
                    tp = t % 2

                    def transp(eng, tp=tp, s=s, c=c, t=t, q=q):
                        W(eng, "dve", ev_ev[("xln", s, c, t)])
                        key = ("xlnT_ev", s, c, t - 2) if t >= 2 else \
                            ((("xlnT_ev",) + sc_of(q - 1) + (t + 2,)) if q >= 1 else None)
                        if key:
                            W(eng, "dve", ev_ev.get(key, 0))
                        pT = pT0 if tp == 0 else pT1
                        for dc in range(DC):
                            inst = eng.transpose(pT[:, dc * P:(dc + 1) * P],
                                                 xln[t][:, dc * P:(dc + 1) * P], ident_b[:])
                        INC(inst, "pe", 1)
                    blk.tensor(transp)
                    ev_pe[("xlnT_pe", s, c, t)] = ctrs["pe"]

                    def evac_xlnT(eng, tp=tp, s=s, c=c, t=t, q=q):
                        W(eng, "pe", ev_pe[("xlnT_pe", s, c, t)])
                        if t == 0 and q >= 1:
                            W(eng, "pe", ev_pe.get(("qkv_done",) + sc_of(q - 1), 0))
                        pT = pT0 if tp == 0 else pT1
                        INC(eng.tensor_copy(
                            xlnT[:].rearrange("p (dc u) -> p dc u", dc=DC)[:, :, t * P:(t + 1) * P],
                            pT[:].rearrange("p (dc u) -> p dc u", dc=DC)), "dve", 1)
                    blk.vector(evac_xlnT)
                    ev_ev[("xlnT_ev", s, c, t)] = ctrs["dve"]

            def a_qkv(q):
                s, c = sc_of(q)
                ev_xlnT_all = ev_ev[("xlnT_ev", s, c, TPC - 1)]
                for oc in range(16):
                    ps = pS0 if oc % 2 == 0 else pS1

                    def qk_mm(eng, oc=oc, ps=ps, s=s, c=c, q=q, ev_xlnT_all=ev_xlnT_all):
                        if oc == 0:
                            W(eng, "dve", max(ev_xlnT_all, ev_wq_done, p0["swv"]))
                        if oc == 1:
                            W(eng, "dve", p0["swb"])
                        W(eng, "act", ev_ev.get(("qk", s, c, oc - 2), 0))
                        if oc < 2 and q >= 1:
                            W(eng, "dve", ev_ev.get(("sc_last",) + sc_of(q - 1), 0))
                        for dc in range(DC):
                            inst = eng.matmul(ps[:],
                                              wq_sb[:, dc * 3 * D + oc * P: dc * 3 * D + (oc + 1) * P],
                                              xlnT[:, dc * 512:(dc + 1) * 512],
                                              start=(dc == 0), stop=(dc == DC - 1))
                        INC(inst, "pe", 1)
                    blk.tensor(qk_mm)
                    ev_pe[("qk", s, c, oc)] = ctrs["pe"]

                    def qk_evac(eng, oc=oc, ps=ps, s=s, c=c):
                        W(eng, "pe", ev_pe[("qk", s, c, oc)])
                        if oc < 8:
                            INC(eng.activation(qT[:, oc * 512:(oc + 1) * 512], ps[:], AF.Identity,
                                               bias=bq32[:, oc:oc + 1],
                                               scale=consts[:, C_QSC:C_QSC + 1]), "act", 1)
                        else:
                            fc = oc - 8
                            INC(eng.activation(kT[:, fc * SEQ + c * 512: fc * SEQ + (c + 1) * 512],
                                               ps[:], AF.Identity,
                                               bias=bq_pc[:, 8 + fc:9 + fc], scale=1.0), "act", 1)
                    blk.scalar(qk_evac)
                    ev_ev[("qk", s, c, oc)] = ctrs["act"]

                for th in range(8):
                    t, h = th // 2, th % 2
                    pv = pA if th % 2 == 0 else pB

                    def v_mm(eng, t=t, h=h, th=th, pv=pv, s=s, c=c, q=q):
                        W(eng, "dve", ev_ev.get(("v", s, c, th - 2), 0))
                        if th < 2 and q >= 1:
                            W(eng, "dve", ev_ev.get(("att_out",) + sc_of(q - 1), 0))
                        for dc in range(DC):
                            eng.matmul(pv[:], xlnT[:, dc * 512 + t * P: dc * 512 + (t + 1) * P],
                                       wq_sb[:, dc * 3 * D + 2 * D + h * 512: dc * 3 * D + 2 * D + (h + 1) * 512],
                                       start=(dc == 0), stop=False)
                        inst = eng.matmul(pv[:], ones_rowb[:], bv_b[:, h * 512:(h + 1) * 512],
                                          start=False, stop=True)
                        INC(inst, "pe", 1)
                    blk.tensor(v_mm)
                    ev_pe[("v", s, c, th)] = ctrs["pe"]

                    def v_evac(eng, t=t, h=h, th=th, pv=pv, s=s, c=c):
                        W(eng, "pe", ev_pe[("v", s, c, th)])
                        gt = 4 * c + t
                        INC(eng.tensor_copy(v_s[:, gt * D + h * 512: gt * D + (h + 1) * 512], pv[:]), "dve", 1)
                    blk.vector(v_evac)
                    ev_ev[("v", s, c, th)] = ctrs["dve"]
                blk.tensor(lambda e: INC(e.drain(), "pe", 1))
                ev_pe[("qkv_done", s, c)] = ctrs["pe"]

            def a_attention(q, fillers):
                s, c = sc_of(q)
                par = c % 2
                ev_v_all = ev_ev[("v", s, c, 7)]
                ev_qk_all = ev_ev[("qk", s, c, 15)]
                for p_ in range(2):
                    qt0 = 4 * c + 2 * p_
                    qt1 = qt0 + 1
                    steps = [(kt, True) for kt in range(qt0 + 1)] + [(qt1, False)]

                    def emit_step(idx, kt, pair):
                        def _sc(eng, kt=kt, pair=pair, idx=idx, s=s, c=c, p_=p_, qt0=qt0):
                            ps = pS0 if idx % 2 == 0 else pS1
                            if idx <= 1:
                                W(eng, "dve", ev_v_all)
                                W(eng, "act", ev_qk_all)
                            W(eng, "dve", ev_ev.get(("sc", s, c, p_, idx - 2), 0))
                            w_ = 256 if pair else 128
                            off = (2 * p_) * 128 if pair else (2 * p_ + 1) * 128
                            for fc in range(DC):
                                inst = eng.matmul(ps[:, 0:w_],
                                                  kT[:, fc * SEQ + kt * P: fc * SEQ + (kt + 1) * P],
                                                  qT[:, fc * 512 + off: fc * 512 + off + w_],
                                                  start=(fc == 0), stop=(fc == DC - 1))
                            INC(inst, "pe", 1)
                        blk.tensor(_sc)
                        ev_pe[("sc", s, c, p_, idx)] = ctrs["pe"]

                        def _mk(eng, kt=kt, pair=pair, idx=idx, s=s, c=c, p_=p_, qt0=qt0):
                            if (s, c, p_, idx) == (0, 0, 0, 0):
                                W2(eng, ev_mask)
                            W(eng, "pe", ev_pe[("sc", s, c, p_, idx)])
                            ps = pS0 if idx % 2 == 0 else pS1
                            w_ = 256 if pair else 128
                            dlt = (qt0 - kt) if pair else 0
                            INC(eng.tensor_tensor(out=scb[idx % 2][:, 0:w_], in0=ps[:, 0:w_],
                                                  in1=maskT[:, dlt * P: dlt * P + w_], op=A.mult), "dve", 1)
                        blk.vector(_mk)
                        ev_ev[("sc", s, c, p_, idx)] = ctrs["dve"]
                        if fillers:
                            a_emit_units(fillers, 1)
                        else:
                            u = p0_unit()
                            if u is not None:
                                u[1]()
                        return (kt, pair, idx)

                    def emit_av(step):
                        kt, pair, idx = step

                        def _av(eng, kt=kt, pair=pair, idx=idx, s=s, c=c, p_=p_, qt0=qt0, qt1=qt1):
                            W(eng, "dve", ev_ev[("sc", s, c, p_, idx)])
                            if kt == 0 and p_ >= 1:
                                W(eng, "dve", ev_ev.get(("att_out_p", s, c, p_ - 1), 0))
                            sb = scb[idx % 2]
                            if pair:
                                for qi in range(2):
                                    bk = (pA, pB) if qi == 0 else (pC, pD)
                                    qtl = qt0 + qi
                                    for h in range(2):
                                        inst = eng.matmul(bk[h][:], sb[:, qi * P:(qi + 1) * P],
                                                          v_s[:, kt * D + h * 512: kt * D + (h + 1) * 512],
                                                          start=(kt == 0), stop=(kt == qtl))
                                INC(inst, "pe", 1)
                            else:
                                for h in range(2):
                                    inst = eng.matmul((pC, pD)[h][:], sb[:, 0:P],
                                                      v_s[:, kt * D + h * 512: kt * D + (h + 1) * 512],
                                                      start=False, stop=True)
                                INC(inst, "pe", 1)
                        blk.tensor(_av)
                        ev_pe[("av", s, c, p_, idx)] = ctrs["pe"]

                    pend = None
                    for idx, (kt, pair) in enumerate(steps):
                        st_ = emit_step(idx, kt, pair)
                        if pend is not None:
                            emit_av(pend)
                        pend = st_
                    emit_av(pend)
                    ev_pe[("att_pair", s, c, p_)] = ctrs["pe"]
                    if p_ == 1:
                        ev_ev[("sc_last", s, c)] = ctrs["dve"]

                    for qi in range(2):
                        for h in range(2):
                            bk = (pA, pB, pC, pD)[2 * qi + h]

                            def att_evac(eng, qi=qi, h=h, bk=bk, p_=p_, s=s, c=c, par=par, q=q):
                                W(eng, "pe", ev_pe[("att_pair", s, c, p_)])
                                if qi == 0 and h == 0:
                                    pkey = (s, c, p_ - 1) if p_ >= 1 else \
                                        ((sc_of(q - 1) + (1,)) if q >= 1 else None)
                                    if pkey:
                                        W2(eng, ev_x2st.get(pkey, None))
                                tl = 2 * p_ + qi
                                INC(eng.scalar_tensor_tensor(
                                    out=x2c[:, qi * D + h * 512: qi * D + (h + 1) * 512],
                                    in0=bk[:], scalar=1.0,
                                    in1=xc[par][:, tl * D + h * 512: tl * D + (h + 1) * 512],
                                    op0=A.mult, op1=A.add), "dve", 1)
                            blk.vector(att_evac)
                    ev_ev[("att_out_p", s, c, p_)] = ctrs["dve"]

                    def st_x2(eng, s=s, c=c, p_=p_):
                        W(eng, "dve", ev_ev[("att_out_p", s, c, p_)])
                        ev_x2st[(s, c, p_)] = DMA(eng, "st", x2v[s, :, 4 * c + 2 * p_:4 * c + 2 * p_ + 2, :],
                                                  x2c[:].rearrange("p (t d) -> p t d", t=2))
                    blk.scalar(st_x2)
                ev_ev[("att_out", s, c)] = ctrs["dve"]
                ev_xc[("free", s, c)] = ctrs["dve"]

            p0_load(1, 0)
            p0_load(1, 1)
            a_ldx(0)
            u0 = a_stats_units(0)
            a_emit_units(u0, 7)
            for t in range(TPC):
                a_emit_units(u0, 1)
                a_transp(0, tiles=[t])
            for q in range(NQ):
                a_qkv(q)
                if q + 1 < NQ:
                    a_ldx(q + 1)
                    fillers = a_stats_units(q + 1)
                else:
                    fillers = []
                a_attention(q, fillers)
                a_emit_units(fillers)
                if p0["k1"] >= NSUB and not p0["sw"]:
                    p0_sw_chain(q)
                    p0_load(2, 0)
                    p0_load(2, 1)
                if q + 1 < NQ:
                    a_transp(q + 1)
                for _ in range(6):
                    u = p0_unit()
                    if u is None:
                        break
                    u[1]()
                if p0["k1"] >= NSUB and not p0["sw"]:
                    p0_sw_chain(q)
                    p0_load(2, 0)
                    p0_load(2, 1)
            while p0["k1"] < NSUB:
                p0_unit()[1]()
            if not p0["sw"]:
                p0_sw_chain(NQ - 1)
                p0_load(2, 0)
                p0_load(2, 1)
            while p0["k2"] < NSUB:
                p0_unit()[1]()
            ev_p0_done = SNAP("st")

            ev_pa_pe = ctrs["pe"]
            ev_pa_dve = ctrs["dve"]
            ev_pa_act = ctrs["act"]
            ev_pa_st = SNAP("st")

        with ExitStack() as pb:
            wg8 = pb.enter_context(nc.sbuf_tensor("wg8", [P, DC * INNER], FP8))
            wu8 = pb.enter_context(nc.sbuf_tensor("wu8", [P, DC * INNER], FP8))
            wd8 = pb.enter_context(nc.sbuf_tensor("wd8", [P, IC * D], FP8))
            x2b = [pb.enter_context(nc.sbuf_tensor(f"x2b{i}", [P, TPC * D], F32)) for i in range(2)]
            outb = pb.enter_context(nc.sbuf_tensor("outb", [P, 2 * D], F32))
            h_f = pb.enter_context(nc.sbuf_tensor("h_f", [P, INNER], F32))
            hq = [pb.enter_context(nc.sbuf_tensor(f"hq{i}", [P, INNER], BF16)) for i in range(2)]
            y16b = pb.enter_context(nc.sbuf_tensor("y16b", [P, INNER], F16))
            xq = [pb.enter_context(nc.sbuf_tensor(f"xq{i}", [P, D], BF16)) for i in range(2)]
            xqT = [pb.enter_context(nc.sbuf_tensor(f"xqT{i}", [P, D], BF16)) for i in range(2)]
            hqT = pb.enter_context(nc.sbuf_tensor("hqT", [P, IC * P], BF16))
            gsl = pb.enter_context(nc.sbuf_tensor("gsl", [P, 512], F32))

            ev_w8 = []
            ev_wd8 = []

            def ld_w8(eng):
                W(eng, "pe", ev_pa_pe)
                W(eng, "dve", ev_pa_dve)
                W(eng, "act", ev_pa_act)
                W2(eng, ev_pa_st)
                W2(eng, ev_p0_done)
                ev_w8.append(DMA(eng, "ld", wg8[:], wg8h))
                ev_w8.append(DMA(eng, "ld", wu8[:], wu8h))
                ev_wd8.append(DMA(eng, "ld", wd8[:], wd8h))
            blk.sync(ld_w8)

            ev_b = {}

            def chunk_of(gi):
                return (gi // (NCH * TPC), (gi // TPC) % NCH)

            def emit_ldx2(s, c):
                par = c % 2
                ci = s * NCH + c

                def ldx2(eng, s=s, c=c, par=par, ci=ci):
                    W2(eng, ev_x2st[(s, c, 0)])
                    W2(eng, ev_x2st[(s, c, 1)])
                    if ci >= 2:
                        pks = chunk_of((ci - 2) * TPC)
                        W(eng, "dve", ev_b.get(("x2free",) + pks, 0))
                    ev_b[("x2ld", s, c)] = DMA(eng, "ld", x2b[par][:].rearrange("p (t d) -> p t d", t=TPC),
                                               x2v[s, :, 4 * c:4 * c + 4, :])
                blk.sync(ldx2)

            stats_pend = []

            def emit_stats(s, c):
                par = c % 2
                pk = (s, c - 2) if c >= 2 else ((s - 1, c + 2) if s >= 1 else None)

                def part_t(t):
                    def c_v1t(eng, s=s, c=c, par=par, pk=pk, t=t):
                        if t == 0:
                            W2(eng, ev_b[("x2ld", s, c)])
                            if pk is not None:
                                W(eng, "dve", ev_b.get(("out_all",) + pk, 0))
                                W(eng, "act", ev_b.get(("act_all",) + pk, 0))
                            prev1 = (s, c - 1) if c >= 1 else ((s - 1, NCH - 1) if s >= 1 else None)
                            if prev1:
                                W(eng, "dve", ev_b.get(("stats",) + prev1, 0))
                            elif ev_ev.get(("ln_done", NSEQ - 1, NCH - 1)):
                                W(eng, "dve", ev_ev[("ln_done", NSEQ - 1, NCH - 1)])
                        xsl = x2b[par][:, t * D:(t + 1) * D]
                        eng.tensor_reduce(out=SC(par, "sum", t), in_=xsl, axis=AX, op=A.add)
                        if t:
                            W(eng, "dve", ev_b[("urB", s, c, t - 1)])
                        eng.scalar_tensor_tensor(out=sqs[:], in0=xsl,
                                                 scalar=1.0, in1=xsl,
                                                 op0=A.mult, op1=A.mult,
                                                 accum_out=SC(par, "sumsq", t))
                        eng.tensor_reduce(out=SC(par, "mx", t), in_=xsl, axis=AX, op=A.max)
                        INC(eng.tensor_reduce(out=SC(par, "mn", t), in_=xsl, axis=AX, op=A.min), "dve", 1)
                        ev_b[("urB", s, c, t)] = ctrs["dve"]
                    def f(t=t):
                        blk.vector(lambda e: c_v1t(e, t=t))
                    return f

                def part_mv():
                    def c_v1b(eng, s=s, c=c, par=par):
                        W(eng, "dve", ev_b[("urB", s, c, TPC - 1)])
                        INC(eng.tensor_scalar(out=SC(par, "m"), in0=SC(par, "sum"),
                                              scalar1=float(2.0 ** -10), scalar2=None, op0=A.mult), "dve", 1)
                        W(eng, "dve", ctrs["dve"])
                        INC(eng.tensor_tensor(out=SC(par, "m2"), in0=SC(par, "m"), in1=SC(par, "m"), op=A.mult), "dve", 1)
                        W(eng, "dve", ctrs["dve"])
                        INC(eng.scalar_tensor_tensor(out=SC(par, "var"), in0=SC(par, "sumsq"),
                                                     scalar=float(2.0 ** -10), in1=SC(par, "m2"),
                                                     op0=A.mult, op1=A.subtract), "dve", 1)
                        ev_b[("mvB", s, c)] = ctrs["dve"]
                    blk.vector(c_v1b)
                    evb1 = ctrs["dve"]

                    def c_a1(eng, par=par, evb1=evb1):
                        W(eng, "dve", evb1)
                        INC(eng.activation(SC(par, "sd"), SC(par, "var"), AF.Sqrt,
                                           bias=consts[:, C_EPS:C_EPS + 1], scale=1.0), "act", 1)
                    blk.scalar(c_a1)
                    ev_b[("sdB", s, c)] = ctrs["act"]

                for t in range(TPC):
                    stats_pend.append(part_t(t))
                stats_pend.append(part_mv)
                stats_pend.append(lambda: emit_stats_tail(s, c))

            def emit_stats_tail(s, c):
                par = c % 2
                evb2 = ev_b[("sdB", s, c)]

                def c_v2(eng, par=par, evb2=evb2, s=s, c=c):
                    W(eng, "act", evb2)
                    W(eng, "dve", ev_b[("mvB", s, c)])
                    eng.tensor_tensor(out=SC(par, "a1"), in0=SC(par, "mx"), in1=SC(par, "m"), op=A.subtract)
                    eng.tensor_tensor(out=SC(par, "a2"), in0=SC(par, "m"), in1=SC(par, "mn"), op=A.subtract)
                    INC(eng.reciprocal(out=SC(par, "r"), in_=SC(par, "sd")), "dve", 1)
                    W(eng, "dve", ctrs["dve"])
                    eng.tensor_tensor(out=SC(par, "Mx"), in0=SC(par, "a1"), in1=SC(par, "a2"), op=A.max)
                    INC(eng.scalar_tensor_tensor(out=SC(par, "mb"), in0=SC(par, "m"), scalar=-1.0,
                                                 in1=SC(par, "r"), op0=A.mult, op1=A.mult), "dve", 1)
                    W(eng, "dve", ctrs["dve"])
                    INC(eng.tensor_tensor(out=SC(par, "Mx"), in0=SC(par, "Mx"), in1=SC(par, "r"), op=A.mult), "dve", 1)
                    W(eng, "dve", ctrs["dve"])
                    INC(eng.tensor_scalar(out=SC(par, "Mx"), in0=SC(par, "Mx"),
                                          scalar1=1e-5, scalar2=None, op0=A.max), "dve", 1)
                    W(eng, "dve", ctrs["dve"])
                    eng.reciprocal(out=SC(par, "s1"), in_=SC(par, "Mx"))
                    INC(eng.tensor_scalar(out=SC(par, "is1"), in0=SC(par, "Mx"),
                                          scalar1=float(1.0 / 127.0), scalar2=None, op0=A.mult), "dve", 1)
                    W(eng, "dve", ctrs["dve"])
                    eng.tensor_scalar(out=SC(par, "s1"), in0=SC(par, "s1"),
                                      scalar1=127.0, scalar2=None, op0=A.mult)
                    eng.tensor_scalar(out=SC(par, "c1"), in0=SC(par, "is1"),
                                      scalar1=swb[:, 1:2], scalar2=None, op0=A.mult)
                    INC(eng.tensor_scalar(out=SC(par, "c2"), in0=SC(par, "is1"),
                                          scalar1=swb[:, 3:4], scalar2=None, op0=A.mult), "dve", 1)
                    W(eng, "dve", ctrs["dve"])
                    eng.tensor_tensor(out=SC(par, "rs1"), in0=SC(par, "r"), in1=SC(par, "s1"), op=A.mult)
                    INC(eng.scalar_tensor_tensor(out=SC(par, "b2"), in0=SC(par, "mb"), scalar=1.0,
                                                 in1=SC(par, "s1"), op0=A.mult, op1=A.mult), "dve", 1)
                    W(eng, "dve", ctrs["dve"])
                    INC(eng.tensor_scalar(out=SC(par, "b2"), in0=SC(par, "b2"),
                                          scalar1=1536.0, scalar2=None, op0=A.add), "dve", 1)
                blk.vector(c_v2)
                ev_b[("stats", s, c)] = ctrs["dve"]

            def emit_quant_xq(gi):
                s, c = chunk_of(gi)
                t = gi % TPC
                par = c % 2
                tp = gi % 2

                def b_v4(eng, par=par, t=t, tp=tp, gi=gi, s=s, c=c):
                    W(eng, "dve", ev_b[("stats", s, c)])
                    W(eng, "dve", ev_b.get(("hq", gi - 2), 0))
                    INC(eng.tensor_scalar(out=y16b[:, 0:D], in0=x2b[par][:, t * D:(t + 1) * D],
                                          scalar1=SC(par, "rs1", t), scalar2=SC(par, "b2", t),
                                          op0=A.mult, op1=A.add), "dve", 1)
                    W(eng, "pe", ev_b.get(("xqT_pe", gi - 2), 0))
                    W(eng, "dve", ctrs["dve"])
                    INC(eng.tensor_scalar(out=xq[tp][:], in0=y16b[:, 0:D],
                                          scalar1=-1536.0, scalar2=None, op0=A.add), "dve", 1)
                blk.vector(b_v4)
                ev_b[("xq", gi)] = ctrs["dve"]

            def emit_txq(gi):
                tp = gi % 2

                def b_txq(eng, tp=tp, gi=gi):
                    W(eng, "dve", ev_b[("xq", gi)])
                    WE(eng, ev_b.get(("hqT_ev", gi - 2, 2), None))
                    for dc in range(DC):
                        inst = eng.transpose(pT0[:, dc * P:(dc + 1) * P],
                                             xq[tp][:, dc * P:(dc + 1) * P], ident_b[:])
                    INC(inst, "pe", 1)
                blk.tensor(b_txq)
                ev_b[("xqT_pe", gi)] = ctrs["pe"]

                def b_exq(eng, tp=tp, gi=gi):
                    W(eng, "pe", ev_b[("xqT_pe", gi)])
                    INC(eng.tensor_copy(xqT[tp][:], pT0[:]), "dve", 1)
                blk.vector(b_exq)
                ev_b[("xqT_ev", gi)] = ("dve", ctrs["dve"])

            def emit_thq_group(gj, grp):
                tp = gj % 2
                for g in ([0], [1], [2, 3])[grp]:
                    pT = pT0 if g % 2 == 0 else pT1

                    def b_thq(eng, g=g, pT=pT, tp=tp, gj=gj):
                        if g == 0:
                            W(eng, "dve", ev_b[("hq", gj)])
                            WE(eng, ev_b.get(("xqT_ev", gj + 2), None))
                        if g >= 2:
                            WE(eng, ev_b.get(("hqT_ev", gj, g - 2), None))
                        for j in range(8):
                            icx = 8 * g + j
                            inst = eng.transpose(pT[:, j * P:(j + 1) * P],
                                                 hq[tp][:, icx * P:(icx + 1) * P], ident_b[:])
                        INC(inst, "pe", 1)
                    blk.tensor(b_thq)
                    ev_b[("hqT_pe", gj, g)] = ctrs["pe"]

                    def b_ehq(eng, g=g, pT=pT, gj=gj):
                        W(eng, "pe", ev_b[("hqT_pe", gj, g)])
                        if g == 0:
                            W(eng, "pe", ev_b.get(("dn_done", gj - 1), 0))
                        INC(eng.tensor_copy(hqT[:, g * 1024:(g + 1) * 1024], pT[:]), "dve", 1)
                    blk.vector(b_ehq)
                    ev_b[("hqT_ev", gj, g)] = ("dve", ctrs["dve"])

            def emit_front(gi):
                s, c = chunk_of(gi)
                t = gi % TPC
                par = c % 2
                tp = gi % 2

                for ic8 in range(8):
                    if ic8 <= 5 and stats_pend:
                        stats_pend.pop(0)()
                    if ic8 == 4 and gi + 1 < NGI:
                        emit_txq(gi + 1)
                    if 5 <= ic8 <= 7 and gi >= 1:
                        emit_thq_group(gi - 1, ic8 - 5)
                    pg = pS0 if ic8 % 2 == 0 else pS1
                    pu = pA if ic8 % 2 == 0 else pB

                    def gu_mm(eng, ic8=ic8, pg=pg, pu=pu, tp=tp, gi=gi):
                        if ic8 == 0:
                            WE(eng, ev_b[("xqT_ev", gi)])
                            if gi == 0:
                                W2(eng, ev_w8)
                        W(eng, "dve", ev_b.get(("h", gi, ic8 - 2),
                                               ev_b.get(("h", gi - 1, 6 + ic8), 0) if ic8 < 2 else 0))
                        for dc in range(DC):
                            eng.matmul(pg[:], xqT[tp][:, dc * P:(dc + 1) * P],
                                       wg8[:, dc * INNER + ic8 * 512: dc * INNER + (ic8 + 1) * 512],
                                       start=(dc == 0), stop=(dc == DC - 1))
                            inst = eng.matmul(pu[:], xqT[tp][:, dc * P:(dc + 1) * P],
                                              wu8[:, dc * INNER + ic8 * 512: dc * INNER + (ic8 + 1) * 512],
                                              start=(dc == 0), stop=(dc == DC - 1))
                        INC(inst, "pe", 1)
                    blk.tensor(gu_mm)
                    ev_b[("gu", gi, ic8)] = ctrs["pe"]

                    def gu_silu(eng, ic8=ic8, pg=pg, par=par, t=t, gi=gi):
                        W(eng, "pe", ev_b[("gu", gi, ic8)])
                        W(eng, "dve", ev_b.get(("h", gi, ic8 - 1),
                                               ev_b.get(("h", gi - 1, 7), 0) if ic8 == 0 else 0))
                        INC(eng.activation(gsl[:], pg[:], AF.Silu,
                                           bias=0.0, scale=SC(par, "c1", t)), "act", 1)
                    blk.scalar(gu_silu)
                    ev_b[("silu", gi, ic8)] = ctrs["act"]

                    def gu_h(eng, ic8=ic8, pu=pu, par=par, t=t, gi=gi):
                        W(eng, "act", ev_b[("silu", gi, ic8)])
                        INC(eng.scalar_tensor_tensor(out=h_f[:, ic8 * 512:(ic8 + 1) * 512],
                                                     in0=pu[:], scalar=SC(par, "c2", t),
                                                     in1=gsl[:], op0=A.mult, op1=A.mult), "dve", 1)
                    blk.vector(gu_h)
                    ev_b[("h", gi, ic8)] = ctrs["dve"]
                blk.tensor(lambda e: INC(e.drain(), "pe", 1))
                ev_b[("gu_done", gi)] = ctrs["pe"]

            def emit_hq(gi):
                s, c = chunk_of(gi)
                t = gi % TPC
                par = c % 2
                tp = gi % 2

                def b_v5(eng, par=par, t=t, tp=tp, gi=gi):
                    W(eng, "dve", ev_b[("h", gi, 7)])
                    INC(eng.tensor_reduce(out=SC(par, "hmx", t), in_=h_f[:], axis=AX,
                                          op=A.max, apply_absolute_value=True), "dve", 1)
                    W(eng, "dve", ctrs["dve"])
                    INC(eng.tensor_scalar(out=SC(par, "c3", t), in0=SC(par, "hmx", t),
                                          scalar1=1e-5, scalar2=swb[:, 6:7], op0=A.max, op1=A.mult), "dve", 1)
                    W(eng, "dve", ctrs["dve"])
                    INC(eng.tensor_scalar(out=SC(par, "hmx", t), in0=SC(par, "hmx", t),
                                          scalar1=1e-5, scalar2=None, op0=A.max), "dve", 1)
                    W(eng, "dve", ctrs["dve"])
                    INC(eng.reciprocal(out=SC(par, "s2", t), in_=SC(par, "hmx", t)), "dve", 1)
                    W(eng, "dve", ctrs["dve"])
                    INC(eng.tensor_scalar(out=SC(par, "s2", t), in0=SC(par, "s2", t),
                                          scalar1=127.0, scalar2=None, op0=A.mult), "dve", 1)
                    W(eng, "dve", ev_b.get(("xq", gi + 1), ev_b.get(("hq", gi - 1), 0)))
                    W(eng, "dve", ctrs["dve"])
                    INC(eng.tensor_scalar(out=y16b[:], in0=h_f[:],
                                          scalar1=SC(par, "s2", t), scalar2=1536.0,
                                          op0=A.mult, op1=A.add), "dve", 1)
                    W(eng, "pe", ev_b.get(("hqT_pe", gi - 2, 3), 0))
                    W(eng, "dve", ctrs["dve"])
                    INC(eng.tensor_scalar(out=hq[tp][:], in0=y16b[:],
                                          scalar1=-1536.0, scalar2=None, op0=A.add), "dve", 1)
                blk.vector(b_v5)
                ev_b[("hq", gi)] = ctrs["dve"]

            def emit_back2(gj):
                s, c = chunk_of(gj)
                t = gj % TPC
                par = c % 2

                def dn_mm(eng, gj=gj):
                    if gj == 0:
                        W2(eng, ev_wd8)
                    WE(eng, ev_b[("hqT_ev", gj, 3)])
                    W(eng, "dve", ev_b.get(("out", gj - 1), 0))
                    for icx in range(IC):
                        for hf, pd_ in ((0, pC), (1, pD)):
                            inst = eng.matmul(pd_[:], hqT[:, icx * P:(icx + 1) * P],
                                              wd8[:, icx * D + hf * 512: icx * D + (hf + 1) * 512],
                                              start=(icx == 0), stop=(icx == IC - 1))
                    INC(inst, "pe", 1)
                blk.tensor(dn_mm)
                ev_b[("dn", gj)] = ctrs["pe"]
                ev_b[("dn_done", gj)] = ctrs["pe"]

                for hf in range(2):
                    pd_ = pC if hf == 0 else pD

                    def out_evac(eng, hf=hf, pd_=pd_, par=par, t=t, gj=gj, s=s, c=c):
                        W(eng, "pe", ev_b[("dn", gj)])
                        if t % 2 == 0 and hf == 0:
                            u = (s * NCH + c) * 2 + t // 2
                            W2(eng, ev_b.get(("outst", u - 1), None))
                        INC(eng.scalar_tensor_tensor(
                            out=outb[:, (t % 2) * D + hf * 512: (t % 2) * D + (hf + 1) * 512],
                            in0=pd_[:], scalar=SC(par, "c3", t),
                            in1=x2b[par][:, t * D + hf * 512: t * D + (hf + 1) * 512],
                            op0=A.mult, op1=A.add), "dve", 1)
                    blk.vector(out_evac)
                ev_b[("out", gj)] = ctrs["dve"]
                if t % 2 == 1:
                    def st_out(eng, s=s, c=c, t=t, gj=gj):
                        W(eng, "dve", ev_b[("out", gj)])
                        ev_b[("outst", (s * NCH + c) * 2 + t // 2)] = DMA(
                            eng, "st", outv[s, :, 4 * c + t - 1:4 * c + t + 1, :],
                            outb[:].rearrange("p (t d) -> p t d", t=2))
                    blk.scalar(st_out)
                if t == TPC - 1:
                    ev_b[("x2free", s, c)] = ctrs["dve"]
                    ev_b[("out_all", s, c)] = ctrs["dve"]
                    blk.scalar(lambda e: INC(e.drain(), "act", 1))
                    ev_b[("act_all", s, c)] = ctrs["act"]

            NGI = NSEQ * NCH * TPC
            emit_ldx2(0, 0)
            emit_stats(0, 0)
            while stats_pend:
                stats_pend.pop(0)()
            emit_quant_xq(0)
            emit_txq(0)
            for gi in range(NGI):
                s, c = chunk_of(gi)
                if gi % TPC == 1 and s * NCH + c + 1 < NSEQ * NCH:
                    sn, cn = chunk_of(gi + TPC)
                    emit_ldx2(sn, cn)
                if gi + 1 < NGI:
                    emit_quant_xq(gi + 1)
                if gi % TPC == 1 and s * NCH + c + 1 < NSEQ * NCH:
                    sn, cn = chunk_of(gi + TPC)
                    emit_stats(sn, cn)
                emit_front(gi)
                emit_hq(gi)
                if gi >= 1:
                    emit_back2(gi - 1)
            for grp in range(3):
                emit_thq_group(NGI - 1, grp)
            emit_back2(NGI - 1)

        def fin(eng):
            for ln_, v in SNAP("st") + SNAP("ld"):
                eng.wait_ge(sems[ln_], v)
        blk.sync(fin)

    return nc


def _host_consts():
    consts = np.zeros((P, 8), dtype=np.float32)
    consts[:, C_EPS] = 1e-5
    consts[:, C_1536] = 1536.0
    consts[:, C_QSC] = 0.03125
    return consts


def _host_mask():
    r = np.arange(P)
    strip = np.zeros((P, NT * P), dtype=np.float64)
    for d in range(NT):
        cc = np.arange(P)
        e = 128.0 * d + cc[None, :] - r[:, None]
        m = ALPHA ** e
        if d == 0:
            m = np.where(cc[None, :] >= r[:, None], m, 0.0)
        strip[:, d * P:(d + 1) * P] = m
    return strip.astype(np.float32)


def kernel(L0_attn_x, L0_attn_W_qkv, L0_attn_b_qkv, L0_ffn_w_gate, L0_ffn_w_up, L0_ffn_w_down):
    x = np.ascontiguousarray(np.asarray(L0_attn_x, dtype=np.float32))
    wqkvT = np.ascontiguousarray(np.asarray(L0_attn_W_qkv, dtype=np.float32).T)
    bq = np.ascontiguousarray(np.asarray(L0_attn_b_qkv, dtype=np.float32))
    wgT = np.ascontiguousarray(np.asarray(L0_ffn_w_gate, dtype=np.float32).T)
    wuT = np.ascontiguousarray(np.asarray(L0_ffn_w_up, dtype=np.float32).T)
    wdT = np.ascontiguousarray(np.asarray(L0_ffn_w_down, dtype=np.float32).T)
    maskT = _host_mask()
    ident = np.eye(P, dtype=np.float32)
    consts = _host_consts()

    nc = build_nc()
    in_maps = []
    for core in range(8):
        in_maps.append({
            "x": np.ascontiguousarray(x[core * NSEQ:(core + 1) * NSEQ]),
            "wqkvT": wqkvT, "bq": bq, "wgT": wgT, "wuT": wuT, "wdT": wdT,
            "maskT": maskT, "ident": ident, "consts": consts,
        })
    res = run_bass_kernel_spmd(nc, in_maps, core_ids=list(range(8)))
    return np.concatenate([np.asarray(r["out"]) for r in res.results], axis=0)


# version 30 (final)
# speedup vs baseline: 1.0306x; 1.0306x over previous
import numpy as np
from contextlib import ExitStack

import concourse.bass as bass
import concourse.mybir as mybir
from concourse.alu_op_type import AluOpType as A
from concourse.bass_utils import run_bass_kernel_spmd

F32 = mybir.dt.float32
BF16 = mybir.dt.bfloat16
F16 = mybir.dt.float16
FP8 = mybir.dt.float8e4
AF = mybir.ActivationFunctionType
AX = mybir.AxisListType.X

P = 128
D = 1024
DC = D // P
SEQ = 2048
NT = SEQ // P
NCH = 4
TPC = 4
NSEQ = 4
INNER = 4096
IC = INNER // P
ALPHA = 0.99

C_EPS, C_1536, C_QSC = 0, 1, 2


def build_nc():
    nc = bass.Bass("TRN2", target_bir_lowering=False)

    x_in = nc.dram_tensor("x", [NSEQ, SEQ, D], F32, kind="ExternalInput").ap()
    wqkvT = nc.dram_tensor("wqkvT", [D, 3 * D], F32, kind="ExternalInput").ap()
    bq_in = nc.dram_tensor("bq", [3 * D], F32, kind="ExternalInput").ap()
    wgT = nc.dram_tensor("wgT", [D, INNER], F32, kind="ExternalInput").ap()
    wuT = nc.dram_tensor("wuT", [D, INNER], F32, kind="ExternalInput").ap()
    wdT = nc.dram_tensor("wdT", [INNER, D], F32, kind="ExternalInput").ap()
    maskT_in = nc.dram_tensor("maskT", [P, NT * P], F32, kind="ExternalInput").ap()
    ident_in = nc.dram_tensor("ident", [P, P], F32, kind="ExternalInput").ap()
    consts_in = nc.dram_tensor("consts", [P, 8], F32, kind="ExternalInput").ap()

    out = nc.dram_tensor("out", [NSEQ, SEQ, D], F32, kind="ExternalOutput").ap()
    x2s = nc.dram_tensor("x2s", [NSEQ, SEQ, D], F32, kind="ExternalOutput").ap()
    wg8h = nc.dram_tensor("wg8h", [P, DC * INNER], FP8, kind="ExternalOutput").ap()
    wu8h = nc.dram_tensor("wu8h", [P, DC * INNER], FP8, kind="ExternalOutput").ap()
    wd8h = nc.dram_tensor("wd8h", [P, IC * D], FP8, kind="ExternalOutput").ap()

    xv = x_in.rearrange("s (t p) d -> s p t d", p=P)
    outv = out.rearrange("s (t p) d -> s p t d", p=P)
    x2v = x2s.rearrange("s (t p) d -> s p t d", p=P)
    wqkvTv = wqkvT.rearrange("(c p) o -> c p o", p=P)
    wgTv = wgT.rearrange("(c p) o -> c p o", p=P)
    wuTv = wuT.rearrange("(c p) o -> c p o", p=P)
    wdTv = wdT.rearrange("(c p) o -> c p o", p=P)
    bqv = bq_in.rearrange("(c p) -> p c", p=P)
    bv_v = bq_in.rearrange("(a o) -> a o", a=3)[2:3, :]

    sems = {}
    ctrs = {}

    def INC(inst, name, k=1):
        inst.then_inc(sems[name], k)
        ctrs[name] += k
        return ctrs[name]

    def W(eng, name, v):
        if v > 0:
            eng.wait_ge(sems[name], v)

    LD_LANES = ["ld0", "ld1", "ld2", "ld3"]
    GP_LANE = "gld"
    ST_LANES = ["st0", "st1", "st2"]
    lane_rr = {"ld": 0, "st": 0}

    with ExitStack() as ctx:
        for s_ in ("pe", "dve", "act", GP_LANE, *LD_LANES, *ST_LANES):
            sems[s_] = ctx.enter_context(nc.semaphore("sem_" + s_))
            ctrs[s_] = 0

        def DMA(eng, kind, out_, in_, **kw):
            lanes = LD_LANES if kind == "ld" else ST_LANES
            i = lane_rr[kind]
            lane_rr[kind] = (i + 1) % len(lanes)
            ln_ = lanes[i]
            W(eng, ln_, ctrs[ln_])
            eng.dma_start(out=out_, in_=in_, **kw).then_inc(sems[ln_], 16)
            ctrs[ln_] += 16
            return (ln_, ctrs[ln_])

        def WE(eng, ev):
            if ev:
                W(eng, ev[0], ev[1])

        def W2(eng, ev):
            if ev:
                if isinstance(ev, list):
                    for e in ev:
                        W(eng, e[0], e[1])
                else:
                    W(eng, ev[0], ev[1])

        def SNAP(kind):
            lanes = LD_LANES if kind == "ld" else ST_LANES
            return [(ln_, ctrs[ln_]) for ln_ in lanes if ctrs[ln_] > 0]

        blk = ctx.enter_context(nc.Block())

        consts = ctx.enter_context(nc.sbuf_tensor("consts_sb", [P, 8], F32))
        ident_b = ctx.enter_context(nc.sbuf_tensor("ident_b", [P, P], BF16))
        bq_pc = ctx.enter_context(nc.sbuf_tensor("bq_pc", [P, 24], F32))
        bq32 = ctx.enter_context(nc.sbuf_tensor("bq32", [P, 8], F32))
        bv_b = ctx.enter_context(nc.sbuf_tensor("bv_b", [1, D], BF16))
        ones_col = ctx.enter_context(nc.sbuf_tensor("ones_col", [P, 1], F32))
        ones_rowf = ctx.enter_context(nc.sbuf_tensor("ones_rowf", [1, P], F32))
        ones_rowb = ctx.enter_context(nc.sbuf_tensor("ones_rowb", [1, P], BF16))
        swb = ctx.enter_context(nc.sbuf_tensor("swb", [P, 8], F32))
        sw_pack = ctx.enter_context(nc.sbuf_tensor("sw_pack", [1, 8], F32))
        stt_ = ctx.enter_context(nc.sbuf_tensor("stats", [P, 224], F32))
        sqs = ctx.enter_context(nc.sbuf_tensor("sqs", [P, D], F32))

        NSTAT = 24

        def SC(par, name, t=None, n=4):
            base = 32 + par * NSTAT * 4 + STAT_OFF[name] * 4
            if t is None:
                return stt_[:, base:base + n]
            return stt_[:, base + t:base + t + 1]
        STAT_OFF = {k: i for i, k in enumerate(
            ["sum", "sumsq", "m", "m2", "var", "sd", "r", "mb", "mx", "mn",
             "a1", "a2", "Mx", "s1", "is1", "rs1", "b2", "c1", "c2",
             "hmx", "s2", "is2", "c3"])}

        pT0 = ctx.enter_context(nc.psum_tensor("pT0", [P, 1024], BF16))
        pT1 = ctx.enter_context(nc.psum_tensor("pT1", [P, 1024], BF16))
        pS0 = ctx.enter_context(nc.psum_tensor("pS0", [P, 512], F32))
        pS1 = ctx.enter_context(nc.psum_tensor("pS1", [P, 512], F32))
        pA = ctx.enter_context(nc.psum_tensor("pA", [P, 512], F32))
        pB = ctx.enter_context(nc.psum_tensor("pB", [P, 512], F32))
        pC = ctx.enter_context(nc.psum_tensor("pC", [P, 512], F32))
        pD = ctx.enter_context(nc.psum_tensor("pD", [P, 512], F32))

        ev_init = []

        def init_loads(eng):
            ev_init.append(DMA(eng, "ld", consts[:], consts_in))
            ev_init.append(DMA(eng, "ld", bq_pc[:], bqv, allow_slow_non_contiguous=True))
            ev_init.append(DMA(eng, "ld", sqs[0:1, 0:D], bv_v))
        blk.sync(init_loads)

        def init_gp(eng):
            eng.dma_start(out=ident_b[:], in_=ident_in).then_inc(sems[GP_LANE], 16)
            ctrs[GP_LANE] += 16
            ev_init.append((GP_LANE, ctrs[GP_LANE]))
        blk.gpsimd(init_gp)

        def init_v(eng):
            eng.memset(ones_col[:], 1.0)
            eng.memset(ones_rowf[:], 1.0)
            eng.memset(ones_rowb[:], 1.0)
            W2(eng, ev_init)
            eng.tensor_copy(bv_b[:], sqs[0:1, 0:D])
            INC(eng.tensor_scalar(out=bq32[:], in0=bq_pc[:, 0:8], scalar1=0.03125,
                                  scalar2=None, op0=A.mult), "dve", 1)
        blk.vector(init_v)
        ev_init_v = ctrs["dve"]

        with ExitStack() as pa:
            maskT = pa.enter_context(nc.sbuf_tensor("maskT_sb", [P, NT * P], F32))
            wbuf = [pa.enter_context(nc.sbuf_tensor(f"wbuf{i}", [P, D], F32)) for i in range(2)]
            y16p = pa.enter_context(nc.sbuf_tensor("y16p", [P, D], F16))
            y16q = pa.enter_context(nc.sbuf_tensor("y16q", [P, D], F16))
            q8b = [pa.enter_context(nc.sbuf_tensor(f"q8_{i}", [P, D], FP8)) for i in range(2)]
            absum = pa.enter_context(nc.sbuf_tensor("absum", [P, 96], F32))
            wq_sb = pa.enter_context(nc.sbuf_tensor("wq_sb", [P, DC * 3 * D], BF16))
            kT = pa.enter_context(nc.sbuf_tensor("kT", [P, DC * SEQ], BF16))
            v_s = pa.enter_context(nc.sbuf_tensor("v_s", [P, NT * D], BF16))
            qT = pa.enter_context(nc.sbuf_tensor("qT", [P, DC * 512], BF16))
            xlnT = pa.enter_context(nc.sbuf_tensor("xlnT", [P, DC * 512], BF16))
            xln = [pa.enter_context(nc.sbuf_tensor(f"xln{i}", [P, D], BF16)) for i in range(4)]
            xc = [pa.enter_context(nc.sbuf_tensor(f"xc{i}", [P, TPC * D], F32)) for i in range(2)]
            x2c = pa.enter_context(nc.sbuf_tensor("x2c", [P, 2 * D], F32))
            scb = [pa.enter_context(nc.sbuf_tensor(f"scb{i}", [P, 256], BF16)) for i in range(2)]

            ev_wq = {}
            for c in range(DC):
                lev = []

                def ldw(eng, c=c, lev=lev):
                    if c >= 2:
                        W(eng, "dve", ev_wq.get(c - 2, 0))
                    lev.append(DMA(eng, "ld", xc[c % 2][:, 0:3 * D], wqkvTv[c]))
                blk.sync(ldw)

                def cvw(eng, c=c, lev=lev):
                    W2(eng, lev)
                    INC(eng.tensor_copy(wq_sb[:, c * 3 * D:(c + 1) * 3 * D], xc[c % 2][:, 0:3 * D]), "dve", 1)
                blk.vector(cvw)
                ev_wq[c] = ctrs["dve"]
            ev_wq_done = ctrs["dve"]
            ev_mask = []
            blk.sync(lambda e: ev_mask.append(DMA(e, "ld", maskT[:], maskT_in)))

            ev_xc = {}
            ev_x2st = {}
            ev_pe = {}
            ev_ev = {}
            NQ = NSEQ * NCH

            NSUB = 96

            wg8hv = wg8h.rearrange("p (i e) -> p i e", i=8)
            wu8hv = wu8h.rearrange("p (i e) -> p i e", i=8)

            def p0_view(k):
                if k < 32:
                    wi, dc, qt = 0, k // 4, k % 4
                    return (wgTv[dc][:, qt * D:(qt + 1) * D], wi,
                            wg8hv[:, 2 * qt:2 * qt + 2, dc * 512:(dc + 1) * 512])
                if k < 64:
                    k -= 32
                    wi, dc, qt = 1, k // 4, k % 4
                    return (wuTv[dc][:, qt * D:(qt + 1) * D], wi,
                            wu8hv[:, 2 * qt:2 * qt + 2, dc * 512:(dc + 1) * 512])
                ic = k - 64
                return (wdTv[ic], 2,
                        wd8h[:, ic * D:(ic + 1) * D].rearrange("p (a c) -> p a c", a=2))

            p0 = {"k1": 0, "k2": 0, "sw": False, "ld1": {}, "ld2": {}, "red": {},
                  "q8": {}, "st": {}, "swv": 0, "swb": 0}

            def p0_load(pass_, k):
                if k >= NSUB:
                    return
                tv = p0_view(k)[0]
                lds = p0["ld1"] if pass_ == 1 else p0["ld2"]

                def ld(eng, k=k, tv=tv, pass_=pass_):
                    if pass_ == 1:
                        W(eng, "dve", p0["red"].get(k - 2, 0))
                    else:
                        W(eng, "dve", p0["q8"].get(k - 2, p0["red"].get(NSUB - 2 + (k % 2), 0)))
                    lds[k] = [DMA(eng, "ld", wbuf[k % 2][:, 0:512], tv[:, 0:512]),
                              DMA(eng, "ld", wbuf[k % 2][:, 512:D], tv[:, 512:D])]
                blk.sync(ld)

            def p0_unit():
                if p0["k1"] < NSUB:
                    k = p0["k1"]
                    p0["k1"] += 1

                    def f(k=k):
                        def red(eng, k=k):
                            W2(eng, p0["ld1"][k])
                            INC(eng.tensor_reduce(out=absum[:, k:k + 1], in_=wbuf[k % 2][:],
                                                  axis=AX, op=A.add,
                                                  apply_absolute_value=True), "dve", 1)
                            p0["red"][k] = ctrs["dve"]
                        blk.vector(red)
                        p0_load(1, k + 2)
                    return ("emit", f)
                if not p0["sw"]:
                    return None
                if p0["k2"] < NSUB:
                    k = p0["k2"]
                    p0["k2"] += 1

                    def f(k=k):
                        tv, wi, dap = p0_view(k)

                        def quant(eng, k=k, wi=wi):
                            W2(eng, p0["ld2"][k])
                            W(eng, "dve", p0["swb"])
                            W(eng, "dve", p0["q8"].get(k - 1, 0))
                            INC(eng.tensor_scalar(out=y16p[:], in0=wbuf[k % 2][:],
                                                  scalar1=swb[:, 2 * wi:2 * wi + 1], scalar2=1536.0,
                                                  op0=A.mult, op1=A.add), "dve", 1)
                            W(eng, "dve", ctrs["dve"])
                            INC(eng.tensor_scalar(out=y16q[:], in0=y16p[:], scalar1=-1536.0,
                                                  scalar2=-1.0, op0=A.add, op1=A.max), "dve", 1)
                            W2(eng, p0["st"].get(k - 2, None))
                            W(eng, "dve", ctrs["dve"])
                            INC(eng.tensor_scalar(out=q8b[k % 2][:], in0=y16q[:], scalar1=1.0,
                                                  scalar2=None, op0=A.min), "dve", 1)
                            p0["q8"][k] = ctrs["dve"]
                        blk.vector(quant)

                        def stq(eng, k=k, dap=dap):
                            W(eng, "dve", p0["q8"][k])
                            W(eng, GP_LANE, ctrs[GP_LANE])
                            eng.dma_start(out=dap,
                                          in_=q8b[k % 2][:].rearrange("p (a c) -> p a c", a=2)
                                          ).then_inc(sems[GP_LANE], 16)
                            ctrs[GP_LANE] += 16
                            p0["st"][k] = (GP_LANE, ctrs[GP_LANE])
                        blk.gpsimd(stq)
                        p0_load(2, k + 2)
                    return ("emit", f)
                return None

            def p0_sw_chain(q):
                def sw_pe(eng, q=q):
                    W(eng, "dve", p0["red"][NSUB - 1])
                    W(eng, "dve", ev_ev.get(("sc_last",) + sc_of(q), 0))
                    INC(eng.matmul(pS0[0:1, 0:96], ones_col[:], absum[:], start=True, stop=True), "pe", 1)
                blk.tensor(sw_pe)
                ev_red = ctrs["pe"]

                def sw_v(eng):
                    W(eng, "pe", ev_red)
                    for wi in range(3):
                        eng.tensor_reduce(out=sw_pack[0:1, 2 * wi + 1:2 * wi + 2],
                                          in_=pS0[0:1, 32 * wi:32 * wi + 32], axis=AX, op=A.add)
                    eng.drain()
                    for wi in range(3):
                        eng.tensor_scalar(out=sw_pack[0:1, 2 * wi + 1:2 * wi + 2],
                                          in0=sw_pack[0:1, 2 * wi + 1:2 * wi + 2],
                                          scalar1=float(2.0 ** -22), scalar2=1e-5,
                                          op0=A.mult, op1=A.max)
                    eng.drain()
                    for wi in range(3):
                        eng.reciprocal(out=sw_pack[0:1, 2 * wi:2 * wi + 1],
                                       in_=sw_pack[0:1, 2 * wi + 1:2 * wi + 2])
                    eng.memset(sw_pack[0:1, 7:8], 0.0)
                    eng.tensor_scalar(out=sw_pack[0:1, 6:7], in0=sw_pack[0:1, 5:6],
                                      scalar1=float(1.0 / 127.0), scalar2=None, op0=A.mult)
                    INC(eng.drain(), "dve", 1)
                blk.vector(sw_v)
                p0["swv"] = ctrs["dve"]

                def sw_bc(eng):
                    W(eng, "dve", p0["swv"])
                    INC(eng.matmul(pS1[:, 0:8], ones_rowf[:], sw_pack[:], start=True, stop=True), "pe", 1)
                blk.tensor(sw_bc)
                ev_swb_pe = ctrs["pe"]

                def sw_cp(eng):
                    W(eng, "pe", ev_swb_pe)
                    INC(eng.tensor_copy(swb[:], pS1[:, 0:8]), "dve", 1)
                blk.vector(sw_cp)
                p0["swb"] = ctrs["dve"]
                p0["sw"] = True

            def sc_of(q):
                return (q // NCH, q % NCH)

            def a_ldx(q):
                s, c = sc_of(q)
                par = c % 2

                def ldx(eng, s=s, c=c, par=par, q=q):
                    if q >= 2:
                        W(eng, "dve", ev_xc.get(("free",) + sc_of(q - 2), 0))
                    elif s == 0 and c < 2:
                        W(eng, "dve", ev_wq_done)
                    ev_xc[("ld", s, c)] = DMA(eng, "ld", xc[par][:].rearrange("p (t d) -> p t d", t=TPC),
                                              xv[s, :, 4 * c:4 * c + 4, :])
                blk.sync(ldx)

            def a_stats_units(q):
                s, c = sc_of(q)
                par = c % 2
                pk = sc_of(q - 2) if q >= 2 else None
                units = []

                def u_red(t, q=q, s=s, c=c, par=par, pk=pk):
                    def f(eng, t=t):
                        if t == 0:
                            W2(eng, ev_xc[("ld", s, c)])
                            if pk:
                                W(eng, "dve", ev_ev.get(("ln_done",) + pk, 0))
                                W(eng, "act", ev_ev.get(("sd",) + pk, 0))
                            prev1 = sc_of(q - 1) if q >= 1 else None
                            if prev1:
                                W(eng, "dve", ev_ev.get(("ln_done",) + prev1, 0))
                        xsl = xc[par][:, t * D:(t + 1) * D]
                        eng.tensor_reduce(out=SC(par, "sum", t), in_=xsl, axis=AX, op=A.add)
                        if t:
                            W(eng, "dve", ev_ev[("ur", s, c, t - 1)])
                        INC(eng.scalar_tensor_tensor(out=sqs[:], in0=xsl,
                                                     scalar=1.0, in1=xsl,
                                                     op0=A.mult, op1=A.mult,
                                                     accum_out=SC(par, "sumsq", t)), "dve", 1)
                        ev_ev[("ur", s, c, t)] = ctrs["dve"]
                    return f
                for t in range(TPC):
                    units.append(("dve", u_red(t)))

                def u_mv(eng, par=par, s=s, c=c):
                    W(eng, "dve", ev_ev[("ur", s, c, TPC - 1)])
                    INC(eng.tensor_scalar(out=SC(par, "m"), in0=SC(par, "sum"),
                                          scalar1=float(2.0 ** -10), scalar2=None, op0=A.mult), "dve", 1)
                    W(eng, "dve", ctrs["dve"])
                    INC(eng.tensor_tensor(out=SC(par, "m2"), in0=SC(par, "m"), in1=SC(par, "m"), op=A.mult), "dve", 1)
                    W(eng, "dve", ctrs["dve"])
                    INC(eng.scalar_tensor_tensor(out=SC(par, "var"), in0=SC(par, "sumsq"),
                                                 scalar=float(2.0 ** -10), in1=SC(par, "m2"),
                                                 op0=A.mult, op1=A.subtract), "dve", 1)
                units.append(("dve", u_mv))

                def u_sqrt_emit():
                    ev_var = ctrs["dve"]

                    def u_sqrt(eng, par=par, ev_var=ev_var):
                        W(eng, "dve", ev_var)
                        INC(eng.activation(SC(par, "sd"), SC(par, "var"), AF.Sqrt,
                                           bias=consts[:, C_EPS:C_EPS + 1], scale=1.0), "act", 1)
                    blk.scalar(u_sqrt)
                    ev_ev[("sd", s, c)] = ctrs["act"]
                units.append(("emit", u_sqrt_emit))

                def u_rmb(eng, par=par, s=s, c=c):
                    W(eng, "act", ev_ev[("sd", s, c)])
                    INC(eng.reciprocal(out=SC(par, "r"), in_=SC(par, "sd")), "dve", 1)
                    W(eng, "dve", ctrs["dve"])
                    INC(eng.scalar_tensor_tensor(out=SC(par, "mb"), in0=SC(par, "m"),
                                                 scalar=-1.0, in1=SC(par, "r"),
                                                 op0=A.mult, op1=A.mult), "dve", 1)
                    ev_ev[("rmb", s, c)] = ctrs["dve"]
                units.append(("dve", u_rmb))

                def u_xln(t, q=q, s=s, c=c, par=par):
                    def f(eng, t=t):
                        if t == 0:
                            W(eng, "dve", ev_ev[("rmb", s, c)])
                        if q >= 1:
                            W(eng, "pe", ev_pe.get(("xlnT_pe",) + sc_of(q - 1) + (t,), 0))
                        INC(eng.tensor_scalar(out=xln[t][:], in0=xc[par][:, t * D:(t + 1) * D],
                                              scalar1=SC(par, "r", t), scalar2=SC(par, "mb", t),
                                              op0=A.mult, op1=A.add), "dve", 1)
                        ev_ev[("xln", s, c, t)] = ctrs["dve"]
                        if t == TPC - 1:
                            ev_ev[("ln_done", s, c)] = ctrs["dve"]
                    return f
                for t in range(TPC):
                    units.append(("dve", u_xln(t)))
                return units

            def a_emit_units(units, n=99):
                k = 0
                while units and k < n:
                    kind, f = units.pop(0)
                    if kind == "emit":
                        f()
                    else:
                        blk.vector(f)
                    k += 1

            def a_transp(q, tiles=None):
                s, c = sc_of(q)
                for t in (tiles if tiles is not None else range(TPC)):
                    tp = t % 2

                    def transp(eng, tp=tp, s=s, c=c, t=t, q=q):
                        W(eng, "dve", ev_ev[("xln", s, c, t)])
                        key = ("xlnT_ev", s, c, t - 2) if t >= 2 else \
                            ((("xlnT_ev",) + sc_of(q - 1) + (t + 2,)) if q >= 1 else None)
                        if key:
                            W(eng, "dve", ev_ev.get(key, 0))
                        pT = pT0 if tp == 0 else pT1
                        for dc in range(DC):
                            inst = eng.transpose(pT[:, dc * P:(dc + 1) * P],
                                                 xln[t][:, dc * P:(dc + 1) * P], ident_b[:])
                        INC(inst, "pe", 1)
                    blk.tensor(transp)
                    ev_pe[("xlnT_pe", s, c, t)] = ctrs["pe"]

                    def evac_xlnT(eng, tp=tp, s=s, c=c, t=t, q=q):
                        W(eng, "pe", ev_pe[("xlnT_pe", s, c, t)])
                        if t == 0 and q >= 1:
                            W(eng, "pe", ev_pe.get(("qkv_done",) + sc_of(q - 1), 0))
                        pT = pT0 if tp == 0 else pT1
                        INC(eng.tensor_copy(
                            xlnT[:].rearrange("p (dc u) -> p dc u", dc=DC)[:, :, t * P:(t + 1) * P],
                            pT[:].rearrange("p (dc u) -> p dc u", dc=DC)), "dve", 1)
                    blk.vector(evac_xlnT)
                    ev_ev[("xlnT_ev", s, c, t)] = ctrs["dve"]

            def a_qkv(q):
                s, c = sc_of(q)
                ev_xlnT_all = ev_ev[("xlnT_ev", s, c, TPC - 1)]
                for oc in range(16):
                    ps = pS0 if oc % 2 == 0 else pS1

                    def qk_mm(eng, oc=oc, ps=ps, s=s, c=c, q=q, ev_xlnT_all=ev_xlnT_all):
                        if oc == 0:
                            W(eng, "dve", max(ev_xlnT_all, ev_wq_done, p0["swv"]))
                        if oc == 1:
                            W(eng, "dve", p0["swb"])
                        W(eng, "act", ev_ev.get(("qk", s, c, oc - 2), 0))
                        if oc < 2 and q >= 1:
                            W(eng, "dve", ev_ev.get(("sc_last",) + sc_of(q - 1), 0))
                        for dc in range(DC):
                            inst = eng.matmul(ps[:],
                                              wq_sb[:, dc * 3 * D + oc * P: dc * 3 * D + (oc + 1) * P],
                                              xlnT[:, dc * 512:(dc + 1) * 512],
                                              start=(dc == 0), stop=(dc == DC - 1))
                        INC(inst, "pe", 1)
                    blk.tensor(qk_mm)
                    ev_pe[("qk", s, c, oc)] = ctrs["pe"]

                    def qk_evac(eng, oc=oc, ps=ps, s=s, c=c):
                        W(eng, "pe", ev_pe[("qk", s, c, oc)])
                        if oc < 8:
                            INC(eng.activation(qT[:, oc * 512:(oc + 1) * 512], ps[:], AF.Identity,
                                               bias=bq32[:, oc:oc + 1],
                                               scale=consts[:, C_QSC:C_QSC + 1]), "act", 1)
                        else:
                            fc = oc - 8
                            INC(eng.activation(kT[:, fc * SEQ + c * 512: fc * SEQ + (c + 1) * 512],
                                               ps[:], AF.Identity,
                                               bias=bq_pc[:, 8 + fc:9 + fc], scale=1.0), "act", 1)
                    blk.scalar(qk_evac)
                    ev_ev[("qk", s, c, oc)] = ctrs["act"]

                for th in range(8):
                    t, h = th // 2, th % 2
                    pv = pA if th % 2 == 0 else pB

                    def v_mm(eng, t=t, h=h, th=th, pv=pv, s=s, c=c, q=q):
                        W(eng, "dve", ev_ev.get(("v", s, c, th - 2), 0))
                        if th < 2 and q >= 1:
                            W(eng, "dve", ev_ev.get(("att_out",) + sc_of(q - 1), 0))
                        for dc in range(DC):
                            eng.matmul(pv[:], xlnT[:, dc * 512 + t * P: dc * 512 + (t + 1) * P],
                                       wq_sb[:, dc * 3 * D + 2 * D + h * 512: dc * 3 * D + 2 * D + (h + 1) * 512],
                                       start=(dc == 0), stop=False)
                        inst = eng.matmul(pv[:], ones_rowb[:], bv_b[:, h * 512:(h + 1) * 512],
                                          start=False, stop=True)
                        INC(inst, "pe", 1)
                    blk.tensor(v_mm)
                    ev_pe[("v", s, c, th)] = ctrs["pe"]

                    def v_evac(eng, t=t, h=h, th=th, pv=pv, s=s, c=c):
                        W(eng, "pe", ev_pe[("v", s, c, th)])
                        gt = 4 * c + t
                        INC(eng.tensor_copy(v_s[:, gt * D + h * 512: gt * D + (h + 1) * 512], pv[:]), "dve", 1)
                    blk.vector(v_evac)
                    ev_ev[("v", s, c, th)] = ctrs["dve"]
                blk.tensor(lambda e: INC(e.drain(), "pe", 1))
                ev_pe[("qkv_done", s, c)] = ctrs["pe"]

            def a_attention(q, fillers):
                s, c = sc_of(q)
                par = c % 2
                ev_v_all = ev_ev[("v", s, c, 7)]
                ev_qk_all = ev_ev[("qk", s, c, 15)]
                for p_ in range(2):
                    qt0 = 4 * c + 2 * p_
                    qt1 = qt0 + 1
                    steps = [(kt, True) for kt in range(qt0 + 1)] + [(qt1, False)]

                    def emit_step(idx, kt, pair):
                        def _sc(eng, kt=kt, pair=pair, idx=idx, s=s, c=c, p_=p_, qt0=qt0):
                            ps = pS0 if idx % 2 == 0 else pS1
                            if idx <= 1:
                                W(eng, "dve", ev_v_all)
                                W(eng, "act", ev_qk_all)
                            W(eng, "dve", ev_ev.get(("sc", s, c, p_, idx - 2), 0))
                            w_ = 256 if pair else 128
                            off = (2 * p_) * 128 if pair else (2 * p_ + 1) * 128
                            for fc in range(DC):
                                inst = eng.matmul(ps[:, 0:w_],
                                                  kT[:, fc * SEQ + kt * P: fc * SEQ + (kt + 1) * P],
                                                  qT[:, fc * 512 + off: fc * 512 + off + w_],
                                                  start=(fc == 0), stop=(fc == DC - 1))
                            INC(inst, "pe", 1)
                        blk.tensor(_sc)
                        ev_pe[("sc", s, c, p_, idx)] = ctrs["pe"]

                        def _mk(eng, kt=kt, pair=pair, idx=idx, s=s, c=c, p_=p_, qt0=qt0):
                            if (s, c, p_, idx) == (0, 0, 0, 0):
                                W2(eng, ev_mask)
                            W(eng, "pe", ev_pe[("sc", s, c, p_, idx)])
                            ps = pS0 if idx % 2 == 0 else pS1
                            w_ = 256 if pair else 128
                            dlt = (qt0 - kt) if pair else 0
                            INC(eng.tensor_tensor(out=scb[idx % 2][:, 0:w_], in0=ps[:, 0:w_],
                                                  in1=maskT[:, dlt * P: dlt * P + w_], op=A.mult), "dve", 1)
                        blk.vector(_mk)
                        ev_ev[("sc", s, c, p_, idx)] = ctrs["dve"]
                        if fillers:
                            a_emit_units(fillers, 1)
                        else:
                            u = p0_unit()
                            if u is not None:
                                u[1]()
                        return (kt, pair, idx)

                    def emit_av(step):
                        kt, pair, idx = step

                        def _av(eng, kt=kt, pair=pair, idx=idx, s=s, c=c, p_=p_, qt0=qt0, qt1=qt1):
                            W(eng, "dve", ev_ev[("sc", s, c, p_, idx)])
                            if kt == 0 and p_ >= 1:
                                W(eng, "dve", ev_ev.get(("att_out_p", s, c, p_ - 1), 0))
                            sb = scb[idx % 2]
                            if pair:
                                for qi in range(2):
                                    bk = (pA, pB) if qi == 0 else (pC, pD)
                                    qtl = qt0 + qi
                                    for h in range(2):
                                        inst = eng.matmul(bk[h][:], sb[:, qi * P:(qi + 1) * P],
                                                          v_s[:, kt * D + h * 512: kt * D + (h + 1) * 512],
                                                          start=(kt == 0), stop=(kt == qtl))
                                INC(inst, "pe", 1)
                            else:
                                for h in range(2):
                                    inst = eng.matmul((pC, pD)[h][:], sb[:, 0:P],
                                                      v_s[:, kt * D + h * 512: kt * D + (h + 1) * 512],
                                                      start=False, stop=True)
                                INC(inst, "pe", 1)
                        blk.tensor(_av)
                        ev_pe[("av", s, c, p_, idx)] = ctrs["pe"]

                    pend = None
                    for idx, (kt, pair) in enumerate(steps):
                        st_ = emit_step(idx, kt, pair)
                        if pend is not None:
                            emit_av(pend)
                        pend = st_
                    emit_av(pend)
                    ev_pe[("att_pair", s, c, p_)] = ctrs["pe"]
                    if p_ == 1:
                        ev_ev[("sc_last", s, c)] = ctrs["dve"]

                    for qi in range(2):
                        for h in range(2):
                            bk = (pA, pB, pC, pD)[2 * qi + h]

                            def att_evac(eng, qi=qi, h=h, bk=bk, p_=p_, s=s, c=c, par=par, q=q):
                                W(eng, "pe", ev_pe[("att_pair", s, c, p_)])
                                if qi == 0 and h == 0:
                                    pkey = (s, c, p_ - 1) if p_ >= 1 else \
                                        ((sc_of(q - 1) + (1,)) if q >= 1 else None)
                                    if pkey:
                                        W2(eng, ev_x2st.get(pkey, None))
                                tl = 2 * p_ + qi
                                INC(eng.scalar_tensor_tensor(
                                    out=x2c[:, qi * D + h * 512: qi * D + (h + 1) * 512],
                                    in0=bk[:], scalar=1.0,
                                    in1=xc[par][:, tl * D + h * 512: tl * D + (h + 1) * 512],
                                    op0=A.mult, op1=A.add), "dve", 1)
                            blk.vector(att_evac)
                    ev_ev[("att_out_p", s, c, p_)] = ctrs["dve"]

                    def st_x2(eng, s=s, c=c, p_=p_):
                        W(eng, "dve", ev_ev[("att_out_p", s, c, p_)])
                        ev_x2st[(s, c, p_)] = DMA(eng, "st", x2v[s, :, 4 * c + 2 * p_:4 * c + 2 * p_ + 2, :],
                                                  x2c[:].rearrange("p (t d) -> p t d", t=2))
                    blk.scalar(st_x2)
                ev_ev[("att_out", s, c)] = ctrs["dve"]
                ev_xc[("free", s, c)] = ctrs["dve"]

            p0_load(1, 0)
            p0_load(1, 1)
            a_ldx(0)
            u0 = a_stats_units(0)
            a_emit_units(u0, 7)
            for t in range(TPC):
                a_emit_units(u0, 1)
                a_transp(0, tiles=[t])
            for q in range(NQ):
                a_qkv(q)
                if q + 1 < NQ:
                    a_ldx(q + 1)
                    fillers = a_stats_units(q + 1)
                else:
                    fillers = []
                a_attention(q, fillers)
                a_emit_units(fillers)
                if p0["k1"] >= NSUB and not p0["sw"]:
                    p0_sw_chain(q)
                    p0_load(2, 0)
                    p0_load(2, 1)
                if q + 1 < NQ:
                    a_transp(q + 1)
                for _ in range(6):
                    u = p0_unit()
                    if u is None:
                        break
                    u[1]()
                if p0["k1"] >= NSUB and not p0["sw"]:
                    p0_sw_chain(q)
                    p0_load(2, 0)
                    p0_load(2, 1)
            while p0["k1"] < NSUB:
                p0_unit()[1]()
            if not p0["sw"]:
                p0_sw_chain(NQ - 1)
                p0_load(2, 0)
                p0_load(2, 1)
            while p0["k2"] < NSUB:
                p0_unit()[1]()
            ev_p0_done = SNAP("st")
            ev_p0_gp = (GP_LANE, ctrs[GP_LANE])

            ev_pa_pe = ctrs["pe"]
            ev_pa_dve = ctrs["dve"]
            ev_pa_act = ctrs["act"]
            ev_pa_st = SNAP("st")

        with ExitStack() as pb:
            wg8 = pb.enter_context(nc.sbuf_tensor("wg8", [P, DC * INNER], FP8))
            wu8 = pb.enter_context(nc.sbuf_tensor("wu8", [P, DC * INNER], FP8))
            wd8 = pb.enter_context(nc.sbuf_tensor("wd8", [P, IC * D], FP8))
            x2b = [pb.enter_context(nc.sbuf_tensor(f"x2b{i}", [P, TPC * D], F32)) for i in range(2)]
            outb = pb.enter_context(nc.sbuf_tensor("outb", [P, 2 * D], F32))
            h_f = pb.enter_context(nc.sbuf_tensor("h_f", [P, INNER], F32))
            hq = [pb.enter_context(nc.sbuf_tensor(f"hq{i}", [P, INNER], BF16)) for i in range(2)]
            y16b = pb.enter_context(nc.sbuf_tensor("y16b", [P, INNER], F16))
            xq = [pb.enter_context(nc.sbuf_tensor(f"xq{i}", [P, D], BF16)) for i in range(2)]
            xqT = [pb.enter_context(nc.sbuf_tensor(f"xqT{i}", [P, D], BF16)) for i in range(2)]
            hqT = pb.enter_context(nc.sbuf_tensor("hqT", [P, IC * P], BF16))
            gsl = pb.enter_context(nc.sbuf_tensor("gsl", [P, 512], F32))

            ev_w8g = [None] * 8
            ev_w8u = [None] * 8
            ev_wd8 = []
            W8 = DC * 512
            wg8v_ = wg8h.rearrange("p (i e) -> p i e", i=8)
            wu8v_ = wu8h.rearrange("p (i e) -> p i e", i=8)

            def ld_w8(eng):
                W(eng, "pe", ev_pa_pe)
                W(eng, "dve", ev_pa_dve)
                W(eng, "act", ev_pa_act)
                W2(eng, ev_pa_st)
                W2(eng, ev_p0_done)
                W2(eng, ev_p0_gp)
                ev_w8g[0] = DMA(eng, "ld", wg8[:, 0:W8], wg8v_[:, 0])
                ev_w8u[0] = DMA(eng, "ld", wu8[:, 0:W8], wu8v_[:, 0])
            blk.sync(ld_w8)

            def ld_w8b(eng):
                for i8 in range(1, 8):
                    ev_w8g[i8] = DMA(eng, "ld", wg8[:, i8 * W8:(i8 + 1) * W8], wg8v_[:, i8])
                    ev_w8u[i8] = DMA(eng, "ld", wu8[:, i8 * W8:(i8 + 1) * W8], wu8v_[:, i8])
                ev_wd8.append(DMA(eng, "ld", wd8[:], wd8h))

            ev_b = {}

            def chunk_of(gi):
                return (gi // (NCH * TPC), (gi // TPC) % NCH)

            def emit_ldx2(s, c):
                par = c % 2
                ci = s * NCH + c

                def ldx2(eng, s=s, c=c, par=par, ci=ci):
                    W2(eng, ev_x2st[(s, c, 0)])
                    W2(eng, ev_x2st[(s, c, 1)])
                    if ci >= 2:
                        pks = chunk_of((ci - 2) * TPC)
                        W(eng, "dve", ev_b.get(("x2free",) + pks, 0))
                    ev_b[("x2ld", s, c)] = DMA(eng, "ld", x2b[par][:].rearrange("p (t d) -> p t d", t=TPC),
                                               x2v[s, :, 4 * c:4 * c + 4, :])
                blk.sync(ldx2)

            stats_pend = []

            def emit_stats(s, c):
                par = c % 2
                pk = (s, c - 2) if c >= 2 else ((s - 1, c + 2) if s >= 1 else None)

                def part_t(t):
                    def c_v1t(eng, s=s, c=c, par=par, pk=pk, t=t):
                        if t == 0:
                            W2(eng, ev_b[("x2ld", s, c)])
                            if pk is not None:
                                W(eng, "dve", ev_b.get(("out_all",) + pk, 0))
                                W(eng, "act", ev_b.get(("act_all",) + pk, 0))
                            prev1 = (s, c - 1) if c >= 1 else ((s - 1, NCH - 1) if s >= 1 else None)
                            if prev1:
                                W(eng, "dve", ev_b.get(("stats",) + prev1, 0))
                            elif ev_ev.get(("ln_done", NSEQ - 1, NCH - 1)):
                                W(eng, "dve", ev_ev[("ln_done", NSEQ - 1, NCH - 1)])
                        xsl = x2b[par][:, t * D:(t + 1) * D]
                        eng.tensor_reduce(out=SC(par, "sum", t), in_=xsl, axis=AX, op=A.add)
                        if t:
                            W(eng, "dve", ev_b[("urB", s, c, t - 1)])
                        eng.scalar_tensor_tensor(out=sqs[:], in0=xsl,
                                                 scalar=1.0, in1=xsl,
                                                 op0=A.mult, op1=A.mult,
                                                 accum_out=SC(par, "sumsq", t))
                        eng.tensor_reduce(out=SC(par, "mx", t), in_=xsl, axis=AX, op=A.max)
                        INC(eng.tensor_reduce(out=SC(par, "mn", t), in_=xsl, axis=AX, op=A.min), "dve", 1)
                        ev_b[("urB", s, c, t)] = ctrs["dve"]
                    def f(t=t):
                        blk.vector(lambda e: c_v1t(e, t=t))
                    return f

                def part_mv():
                    def c_v1b(eng, s=s, c=c, par=par):
                        W(eng, "dve", ev_b[("urB", s, c, TPC - 1)])
                        INC(eng.tensor_scalar(out=SC(par, "m"), in0=SC(par, "sum"),
                                              scalar1=float(2.0 ** -10), scalar2=None, op0=A.mult), "dve", 1)
                        W(eng, "dve", ctrs["dve"])
                        INC(eng.tensor_tensor(out=SC(par, "m2"), in0=SC(par, "m"), in1=SC(par, "m"), op=A.mult), "dve", 1)
                        W(eng, "dve", ctrs["dve"])
                        INC(eng.scalar_tensor_tensor(out=SC(par, "var"), in0=SC(par, "sumsq"),
                                                     scalar=float(2.0 ** -10), in1=SC(par, "m2"),
                                                     op0=A.mult, op1=A.subtract), "dve", 1)
                        ev_b[("mvB", s, c)] = ctrs["dve"]
                    blk.vector(c_v1b)
                    evb1 = ctrs["dve"]

                    def c_a1(eng, par=par, evb1=evb1):
                        W(eng, "dve", evb1)
                        INC(eng.activation(SC(par, "sd"), SC(par, "var"), AF.Sqrt,
                                           bias=consts[:, C_EPS:C_EPS + 1], scale=1.0), "act", 1)
                    blk.scalar(c_a1)
                    ev_b[("sdB", s, c)] = ctrs["act"]

                for t in range(TPC):
                    stats_pend.append(part_t(t))
                stats_pend.append(part_mv)
                stats_pend.append(lambda: emit_stats_tail(s, c))

            def emit_stats_tail(s, c):
                par = c % 2
                evb2 = ev_b[("sdB", s, c)]

                def c_v2(eng, par=par, evb2=evb2, s=s, c=c):
                    W(eng, "act", evb2)
                    W(eng, "dve", ev_b[("mvB", s, c)])
                    eng.tensor_tensor(out=SC(par, "a1"), in0=SC(par, "mx"), in1=SC(par, "m"), op=A.subtract)
                    eng.tensor_tensor(out=SC(par, "a2"), in0=SC(par, "m"), in1=SC(par, "mn"), op=A.subtract)
                    INC(eng.reciprocal(out=SC(par, "r"), in_=SC(par, "sd")), "dve", 1)
                    W(eng, "dve", ctrs["dve"])
                    eng.tensor_tensor(out=SC(par, "Mx"), in0=SC(par, "a1"), in1=SC(par, "a2"), op=A.max)
                    INC(eng.scalar_tensor_tensor(out=SC(par, "mb"), in0=SC(par, "m"), scalar=-1.0,
                                                 in1=SC(par, "r"), op0=A.mult, op1=A.mult), "dve", 1)
                    W(eng, "dve", ctrs["dve"])
                    INC(eng.tensor_tensor(out=SC(par, "Mx"), in0=SC(par, "Mx"), in1=SC(par, "r"), op=A.mult), "dve", 1)
                    W(eng, "dve", ctrs["dve"])
                    INC(eng.tensor_scalar(out=SC(par, "Mx"), in0=SC(par, "Mx"),
                                          scalar1=1e-5, scalar2=None, op0=A.max), "dve", 1)
                    W(eng, "dve", ctrs["dve"])
                    eng.reciprocal(out=SC(par, "s1"), in_=SC(par, "Mx"))
                    INC(eng.tensor_scalar(out=SC(par, "is1"), in0=SC(par, "Mx"),
                                          scalar1=float(1.0 / 127.0), scalar2=None, op0=A.mult), "dve", 1)
                    W(eng, "dve", ctrs["dve"])
                    eng.tensor_scalar(out=SC(par, "s1"), in0=SC(par, "s1"),
                                      scalar1=127.0, scalar2=None, op0=A.mult)
                    eng.tensor_scalar(out=SC(par, "c1"), in0=SC(par, "is1"),
                                      scalar1=swb[:, 1:2], scalar2=None, op0=A.mult)
                    INC(eng.tensor_scalar(out=SC(par, "c2"), in0=SC(par, "is1"),
                                          scalar1=swb[:, 3:4], scalar2=None, op0=A.mult), "dve", 1)
                    W(eng, "dve", ctrs["dve"])
                    eng.tensor_tensor(out=SC(par, "rs1"), in0=SC(par, "r"), in1=SC(par, "s1"), op=A.mult)
                    INC(eng.scalar_tensor_tensor(out=SC(par, "b2"), in0=SC(par, "mb"), scalar=1.0,
                                                 in1=SC(par, "s1"), op0=A.mult, op1=A.mult), "dve", 1)
                    W(eng, "dve", ctrs["dve"])
                    INC(eng.tensor_scalar(out=SC(par, "b2"), in0=SC(par, "b2"),
                                          scalar1=1536.0, scalar2=None, op0=A.add), "dve", 1)
                blk.vector(c_v2)
                ev_b[("stats", s, c)] = ctrs["dve"]

            def emit_quant_xq(gi):
                s, c = chunk_of(gi)
                t = gi % TPC
                par = c % 2
                tp = gi % 2

                def b_v4(eng, par=par, t=t, tp=tp, gi=gi, s=s, c=c):
                    W(eng, "dve", ev_b[("stats", s, c)])
                    W(eng, "dve", ev_b.get(("hq", gi - 2), 0))
                    INC(eng.tensor_scalar(out=y16b[:, 0:D], in0=x2b[par][:, t * D:(t + 1) * D],
                                          scalar1=SC(par, "rs1", t), scalar2=SC(par, "b2", t),
                                          op0=A.mult, op1=A.add), "dve", 1)
                    W(eng, "pe", ev_b.get(("xqT_pe", gi - 2), 0))
                    W(eng, "dve", ctrs["dve"])
                    INC(eng.tensor_scalar(out=xq[tp][:], in0=y16b[:, 0:D],
                                          scalar1=-1536.0, scalar2=None, op0=A.add), "dve", 1)
                blk.vector(b_v4)
                ev_b[("xq", gi)] = ctrs["dve"]

            def emit_txq(gi):
                tp = gi % 2

                def b_txq(eng, tp=tp, gi=gi):
                    W(eng, "dve", ev_b[("xq", gi)])
                    WE(eng, ev_b.get(("hqT_ev", gi - 2, 2), None))
                    for dc in range(DC):
                        inst = eng.transpose(pT0[:, dc * P:(dc + 1) * P],
                                             xq[tp][:, dc * P:(dc + 1) * P], ident_b[:])
                    INC(inst, "pe", 1)
                blk.tensor(b_txq)
                ev_b[("xqT_pe", gi)] = ctrs["pe"]

                def b_exq(eng, tp=tp, gi=gi):
                    W(eng, "pe", ev_b[("xqT_pe", gi)])
                    INC(eng.tensor_copy(xqT[tp][:], pT0[:]), "dve", 1)
                blk.vector(b_exq)
                ev_b[("xqT_ev", gi)] = ("dve", ctrs["dve"])

            def emit_thq_group(gj, grp):
                tp = gj % 2
                for g in ([0], [1], [2, 3])[grp]:
                    pT = pT0 if g % 2 == 0 else pT1

                    def b_thq(eng, g=g, pT=pT, tp=tp, gj=gj):
                        if g == 0:
                            W(eng, "dve", ev_b[("hq", gj)])
                            WE(eng, ev_b.get(("xqT_ev", gj + 2), None))
                        if g >= 2:
                            WE(eng, ev_b.get(("hqT_ev", gj, g - 2), None))
                        for j in range(8):
                            icx = 8 * g + j
                            inst = eng.transpose(pT[:, j * P:(j + 1) * P],
                                                 hq[tp][:, icx * P:(icx + 1) * P], ident_b[:])
                        INC(inst, "pe", 1)
                    blk.tensor(b_thq)
                    ev_b[("hqT_pe", gj, g)] = ctrs["pe"]

                    def b_ehq(eng, g=g, pT=pT, gj=gj):
                        W(eng, "pe", ev_b[("hqT_pe", gj, g)])
                        if g == 0:
                            W(eng, "pe", ev_b.get(("dn_done", gj - 1), 0))
                        INC(eng.tensor_copy(hqT[:, g * 1024:(g + 1) * 1024], pT[:]), "dve", 1)
                    blk.vector(b_ehq)
                    ev_b[("hqT_ev", gj, g)] = ("dve", ctrs["dve"])

            def emit_front(gi):
                s, c = chunk_of(gi)
                t = gi % TPC
                par = c % 2
                tp = gi % 2

                for ic8 in range(8):
                    if ic8 <= 5 and stats_pend:
                        stats_pend.pop(0)()
                    if ic8 == 4 and gi + 1 < NGI:
                        emit_txq(gi + 1)
                    if 5 <= ic8 <= 7 and gi >= 1:
                        emit_thq_group(gi - 1, ic8 - 5)
                    pg = pS0 if ic8 % 2 == 0 else pS1
                    pu = pA if ic8 % 2 == 0 else pB

                    def gu_mm(eng, ic8=ic8, pg=pg, pu=pu, tp=tp, gi=gi):
                        if ic8 == 0:
                            WE(eng, ev_b[("xqT_ev", gi)])
                        if gi == 0:
                            W2(eng, ev_w8g[ic8])
                            W2(eng, ev_w8u[ic8])
                        W(eng, "dve", ev_b.get(("h", gi, ic8 - 2),
                                               ev_b.get(("h", gi - 1, 6 + ic8), 0) if ic8 < 2 else 0))
                        for dc in range(DC):
                            eng.matmul(pg[:], xqT[tp][:, dc * P:(dc + 1) * P],
                                       wg8[:, ic8 * W8 + dc * 512: ic8 * W8 + (dc + 1) * 512],
                                       start=(dc == 0), stop=(dc == DC - 1))
                            inst = eng.matmul(pu[:], xqT[tp][:, dc * P:(dc + 1) * P],
                                              wu8[:, ic8 * W8 + dc * 512: ic8 * W8 + (dc + 1) * 512],
                                              start=(dc == 0), stop=(dc == DC - 1))
                        INC(inst, "pe", 1)
                    blk.tensor(gu_mm)
                    ev_b[("gu", gi, ic8)] = ctrs["pe"]

                    def gu_silu(eng, ic8=ic8, pg=pg, par=par, t=t, gi=gi):
                        W(eng, "pe", ev_b[("gu", gi, ic8)])
                        W(eng, "dve", ev_b.get(("h", gi, ic8 - 1),
                                               ev_b.get(("h", gi - 1, 7), 0) if ic8 == 0 else 0))
                        INC(eng.activation(gsl[:], pg[:], AF.Silu,
                                           bias=0.0, scale=SC(par, "c1", t)), "act", 1)
                    blk.scalar(gu_silu)
                    ev_b[("silu", gi, ic8)] = ctrs["act"]

                    def gu_h(eng, ic8=ic8, pu=pu, par=par, t=t, gi=gi):
                        W(eng, "act", ev_b[("silu", gi, ic8)])
                        INC(eng.scalar_tensor_tensor(out=h_f[:, ic8 * 512:(ic8 + 1) * 512],
                                                     in0=pu[:], scalar=SC(par, "c2", t),
                                                     in1=gsl[:], op0=A.mult, op1=A.mult), "dve", 1)
                    blk.vector(gu_h)
                    ev_b[("h", gi, ic8)] = ctrs["dve"]
                blk.tensor(lambda e: INC(e.drain(), "pe", 1))
                ev_b[("gu_done", gi)] = ctrs["pe"]

            def emit_hq(gi):
                s, c = chunk_of(gi)
                t = gi % TPC
                par = c % 2
                tp = gi % 2

                def b_v5(eng, par=par, t=t, tp=tp, gi=gi):
                    W(eng, "dve", ev_b[("h", gi, 7)])
                    INC(eng.tensor_reduce(out=SC(par, "hmx", t), in_=h_f[:], axis=AX,
                                          op=A.max, apply_absolute_value=True), "dve", 1)
                    W(eng, "dve", ctrs["dve"])
                    INC(eng.tensor_scalar(out=SC(par, "c3", t), in0=SC(par, "hmx", t),
                                          scalar1=1e-5, scalar2=swb[:, 6:7], op0=A.max, op1=A.mult), "dve", 1)
                    W(eng, "dve", ctrs["dve"])
                    INC(eng.tensor_scalar(out=SC(par, "hmx", t), in0=SC(par, "hmx", t),
                                          scalar1=1e-5, scalar2=None, op0=A.max), "dve", 1)
                    W(eng, "dve", ctrs["dve"])
                    INC(eng.reciprocal(out=SC(par, "s2", t), in_=SC(par, "hmx", t)), "dve", 1)
                    W(eng, "dve", ctrs["dve"])
                    INC(eng.tensor_scalar(out=SC(par, "s2", t), in0=SC(par, "s2", t),
                                          scalar1=127.0, scalar2=None, op0=A.mult), "dve", 1)
                    W(eng, "dve", ev_b.get(("xq", gi + 1), ev_b.get(("hq", gi - 1), 0)))
                    W(eng, "dve", ctrs["dve"])
                    INC(eng.tensor_scalar(out=y16b[:], in0=h_f[:],
                                          scalar1=SC(par, "s2", t), scalar2=1536.0,
                                          op0=A.mult, op1=A.add), "dve", 1)
                    W(eng, "pe", ev_b.get(("hqT_pe", gi - 2, 3), 0))
                    W(eng, "dve", ctrs["dve"])
                    INC(eng.tensor_scalar(out=hq[tp][:], in0=y16b[:],
                                          scalar1=-1536.0, scalar2=None, op0=A.add), "dve", 1)
                blk.vector(b_v5)
                ev_b[("hq", gi)] = ctrs["dve"]

            def emit_back2(gj):
                s, c = chunk_of(gj)
                t = gj % TPC
                par = c % 2

                def dn_mm(eng, gj=gj):
                    if gj == 0:
                        W2(eng, ev_wd8)
                    WE(eng, ev_b[("hqT_ev", gj, 3)])
                    W(eng, "dve", ev_b.get(("out", gj - 1), 0))
                    for icx in range(IC):
                        for hf, pd_ in ((0, pC), (1, pD)):
                            inst = eng.matmul(pd_[:], hqT[:, icx * P:(icx + 1) * P],
                                              wd8[:, icx * D + hf * 512: icx * D + (hf + 1) * 512],
                                              start=(icx == 0), stop=(icx == IC - 1))
                    INC(inst, "pe", 1)
                blk.tensor(dn_mm)
                ev_b[("dn", gj)] = ctrs["pe"]
                ev_b[("dn_done", gj)] = ctrs["pe"]

                for hf in range(2):
                    pd_ = pC if hf == 0 else pD

                    def out_evac(eng, hf=hf, pd_=pd_, par=par, t=t, gj=gj, s=s, c=c):
                        W(eng, "pe", ev_b[("dn", gj)])
                        if t % 2 == 0 and hf == 0:
                            u = (s * NCH + c) * 2 + t // 2
                            W2(eng, ev_b.get(("outst", u - 1), None))
                        INC(eng.scalar_tensor_tensor(
                            out=outb[:, (t % 2) * D + hf * 512: (t % 2) * D + (hf + 1) * 512],
                            in0=pd_[:], scalar=SC(par, "c3", t),
                            in1=x2b[par][:, t * D + hf * 512: t * D + (hf + 1) * 512],
                            op0=A.mult, op1=A.add), "dve", 1)
                    blk.vector(out_evac)
                ev_b[("out", gj)] = ctrs["dve"]
                if t % 2 == 1:
                    def st_out(eng, s=s, c=c, t=t, gj=gj):
                        W(eng, "dve", ev_b[("out", gj)])
                        ev_b[("outst", (s * NCH + c) * 2 + t // 2)] = DMA(
                            eng, "st", outv[s, :, 4 * c + t - 1:4 * c + t + 1, :],
                            outb[:].rearrange("p (t d) -> p t d", t=2))
                    blk.scalar(st_out)
                if t == TPC - 1:
                    ev_b[("x2free", s, c)] = ctrs["dve"]
                    ev_b[("out_all", s, c)] = ctrs["dve"]
                    blk.scalar(lambda e: INC(e.drain(), "act", 1))
                    ev_b[("act_all", s, c)] = ctrs["act"]

            NGI = NSEQ * NCH * TPC
            emit_ldx2(0, 0)
            blk.sync(ld_w8b)
            emit_stats(0, 0)
            while stats_pend:
                stats_pend.pop(0)()
            emit_quant_xq(0)
            emit_txq(0)
            for gi in range(NGI):
                s, c = chunk_of(gi)
                if gi % TPC == 1 and s * NCH + c + 1 < NSEQ * NCH:
                    sn, cn = chunk_of(gi + TPC)
                    emit_ldx2(sn, cn)
                if gi + 1 < NGI:
                    emit_quant_xq(gi + 1)
                if gi % TPC == 1 and s * NCH + c + 1 < NSEQ * NCH:
                    sn, cn = chunk_of(gi + TPC)
                    emit_stats(sn, cn)
                emit_front(gi)
                emit_hq(gi)
                if gi >= 1:
                    emit_back2(gi - 1)
            for grp in range(3):
                emit_thq_group(NGI - 1, grp)
            emit_back2(NGI - 1)

        def fin(eng):
            for ln_, v in SNAP("st") + SNAP("ld"):
                eng.wait_ge(sems[ln_], v)
        blk.sync(fin)

    return nc


def _host_consts():
    consts = np.zeros((P, 8), dtype=np.float32)
    consts[:, C_EPS] = 1e-5
    consts[:, C_1536] = 1536.0
    consts[:, C_QSC] = 0.03125
    return consts


def _host_mask():
    r = np.arange(P)
    strip = np.zeros((P, NT * P), dtype=np.float64)
    for d in range(NT):
        cc = np.arange(P)
        e = 128.0 * d + cc[None, :] - r[:, None]
        m = ALPHA ** e
        if d == 0:
            m = np.where(cc[None, :] >= r[:, None], m, 0.0)
        strip[:, d * P:(d + 1) * P] = m
    return strip.astype(np.float32)


def kernel(L0_attn_x, L0_attn_W_qkv, L0_attn_b_qkv, L0_ffn_w_gate, L0_ffn_w_up, L0_ffn_w_down):
    x = np.ascontiguousarray(np.asarray(L0_attn_x, dtype=np.float32))
    wqkvT = np.ascontiguousarray(np.asarray(L0_attn_W_qkv, dtype=np.float32).T)
    bq = np.ascontiguousarray(np.asarray(L0_attn_b_qkv, dtype=np.float32))
    wgT = np.ascontiguousarray(np.asarray(L0_ffn_w_gate, dtype=np.float32).T)
    wuT = np.ascontiguousarray(np.asarray(L0_ffn_w_up, dtype=np.float32).T)
    wdT = np.ascontiguousarray(np.asarray(L0_ffn_w_down, dtype=np.float32).T)
    maskT = _host_mask()
    ident = np.eye(P, dtype=np.float32)
    consts = _host_consts()

    nc = build_nc()
    in_maps = []
    for core in range(8):
        in_maps.append({
            "x": np.ascontiguousarray(x[core * NSEQ:(core + 1) * NSEQ]),
            "wqkvT": wqkvT, "bq": bq, "wgT": wgT, "wuT": wuT, "wdT": wdT,
            "maskT": maskT, "ident": ident, "consts": consts,
        })
    res = run_bass_kernel_spmd(nc, in_maps, core_ids=list(range(8)))
    return np.concatenate([np.asarray(r["out"]) for r in res.results], axis=0)
